# version 05
# baseline (speedup 1.0000x reference)
import numpy as np
import ml_dtypes
import concourse.bass as bass
import concourse.mybir as mybir
from concourse.bass_utils import run_bass_kernel_spmd
from contextlib import ExitStack

F32 = mybir.dt.float32
BF16 = mybir.dt.bfloat16
I8 = mybir.dt.int8
AF = mybir.ActivationFunctionType
ALU = mybir.AluOpType
BF = ml_dtypes.bfloat16

D_MODEL = 2048
NCHUNK = D_MODEL // 128
TOK = 4096
SEQ = 2048
M_GT = 512
M_STAGES = 15
M_DTN = 99


class Emit:
    ENGINES = ("tensor", "vector", "scalar", "gpsimd", "sync")

    class LoopVar:
        def __init__(self):
            self.val = 0

    NRING = 4

    def __init__(self, serial=False):
        self.steps = []
        self.serial = serial
        self.loops = []
        self._loop = None
        self._ring_ctr = {}
        self._ring_last = {}

    def emit(self, engine, fn, deps=(), inc=None):
        if inc is None:
            inc = 16 if fn.__name__ == "dma" else 1
        i = len(self.steps)
        deps = list(deps)
        if self.serial and i > 0:
            deps.append(i - 1)
        st = dict(engine=engine, fn=fn, inc=inc, deps=deps)
        if inc == 16:
            r = self._ring_ctr.get(engine, 0)
            self._ring_ctr[engine] = (r + 1) % self.NRING
            st["ring"] = r
            prev = self._ring_last.get((engine, r))
            if prev is not None:
                deps.append(prev)
            self._ring_last[(engine, r)] = i
        self.steps.append(st)
        return i

    def begin_loop(self, n):
        assert self._loop is None
        lv = Emit.LoopVar()
        self._loop = dict(start=len(self.steps), n=n, var=lv)
        return lv

    def end_loop(self):
        l = self._loop
        l["end"] = len(self.steps)
        self.loops.append(l)
        self._loop = None
        ring_first = {}
        ring_last = {}
        for i in range(l["start"], l["end"]):
            st = self.steps[i]
            if st["inc"] == 16:
                key = (st["engine"], st["ring"])
                ring_first.setdefault(key, i)
                ring_last[key] = i
        for key, fi in ring_first.items():
            self.steps[fi]["deps"].append((ring_last[key], -1))

    def tail(self):
        last = {}
        for i, st in enumerate(self.steps):
            last[self._slot(st)] = i
        return list(last.values())

    def dma(self, engine, out, in_, deps=()):
        def dma(eng):
            return eng.dma_start(out=out, in_=in_)
        return self.emit(engine, dma, deps, inc=16)

    @staticmethod
    def _slot(st):
        if st["inc"] == 16:
            return st["engine"] + "_dma%d" % st["ring"]
        return st["engine"]

    def lower(self, nc, es, prefix=""):
        slots = sorted({self._slot(st) for st in self.steps})
        sems = {sl: es.enter_context(nc.semaphore(prefix + "sem_" + sl))
                for sl in slots}
        in_loop = {}
        for l in self.loops:
            for i in range(l["start"], l["end"]):
                in_loop[i] = l
        for l in self.loops:
            d = {sl: 0 for sl in slots}
            for i in range(l["start"], l["end"]):
                d[self._slot(self.steps[i])] += self.steps[i]["inc"]
            l["delta"] = d
        cum = {}
        run = {sl: 0 for sl in slots}
        for i, st in enumerate(self.steps):
            sl = self._slot(st)
            run[sl] += st["inc"]
            cum[i] = run[sl]
            l = in_loop.get(i)
            if l is not None and i == l["end"] - 1:
                for sl2 in slots:
                    run[sl2] += l["delta"][sl2] * (l["n"] - 1)

        def dep_id(d):
            return d[0] if isinstance(d, tuple) else d

        def dep_off(d):
            return d[1] if isinstance(d, tuple) else 0

        step_at = {}
        for i, st in enumerate(self.steps):
            step_at[(self._slot(st), cum[i])] = i

        know_eng = {e: {sl: 0 for sl in slots} for e in self.ENGINES}
        know_after = [None] * len(self.steps)
        plan = [None] * len(self.steps)
        for i, st in enumerate(self.steps):
            e = st["engine"]
            l = in_loop.get(i)
            waits = []
            if l is None:
                ke = know_eng[e]
                for d in st["deps"]:
                    di = dep_id(d)
                    ds_ = self._slot(self.steps[di])
                    de = self.steps[di]["engine"]
                    if (de == e and self.steps[di]["inc"] == 1
                            and e == "tensor"):
                        continue
                    dl = in_loop.get(di)
                    if dl is not None:
                        v = cum[di] + dl["delta"][ds_] * (dl["n"] - 1)
                    else:
                        v = cum[di]
                    if ke.get(ds_, 0) < v:
                        waits.append((ds_, v))
                ded = {}
                for ds_, v in waits:
                    ded[ds_] = max(ded.get(ds_, 0), v)
                waits = sorted(ded.items())
                for ds_, v in waits:
                    ke[ds_] = max(ke[ds_], v)
                    j = step_at.get((ds_, v))
                    if j is not None and know_after[j] is not None:
                        for s2, v2 in know_after[j].items():
                            if ke[s2] < v2:
                                ke[s2] = v2
                plan[i] = ("plain", waits)
                ka = dict(ke)
                sl = self._slot(st)
                ka[sl] = max(ka[sl], cum[i])
                know_after[i] = ka
            else:
                if i == l["start"]:
                    l["_ke"] = {en: {sl: 0 for sl in slots}
                                for en in self.ENGINES}
                ke2 = l["_ke"][e]
                lw = []
                for d in st["deps"]:
                    di = dep_id(d)
                    off = dep_off(d)
                    ds_ = self._slot(self.steps[di])
                    de = self.steps[di]["engine"]
                    if (de == e and self.steps[di]["inc"] == 1
                            and e == "tensor"):
                        continue
                    if in_loop.get(di) is l:
                        v = cum[di] + off * l["delta"][ds_]
                        if ke2.get(ds_, 0) < v:
                            lw.append((ds_, v, "reg"))
                            ke2[ds_] = v
                    else:
                        v = cum[di]
                        if know_eng[e].get(ds_, 0) < v:
                            lw.append((ds_, v, "static"))
                            know_eng[e][ds_] = v
                plan[i] = ("loop", lw)
                know_after[i] = dict(know_eng[e])
            if l is not None and i == l["end"] - 1:
                for en in self.ENGINES:
                    know_eng[en] = {sl: 0 for sl in slots}

        for l in self.loops:
            l["regseq"] = {}
            for i in range(l["start"], l["end"]):
                st = self.steps[i]
                for w in plan[i][1]:
                    if w[2] == "reg":
                        l["regseq"].setdefault((st["engine"], w[0]), []).append(
                            (i, w[1]))

        def body(eng_name, eng):
            i = 0
            n_steps = len(self.steps)
            while i < n_steps:
                l = in_loop.get(i)
                if l is None:
                    st = self.steps[i]
                    if st["engine"] == eng_name:
                        for ds_, v in plan[i][1]:
                            eng.wait_ge(sems[ds_], v)
                        r = st["fn"](eng)
                        if r is None:
                            r = eng.nop()
                        r.then_inc(sems[self._slot(st)], st["inc"])
                    i += 1
                    continue
                body_ids = [j for j in range(l["start"], l["end"])
                            if self.steps[j]["engine"] == eng_name]
                if body_ids:
                    regs = {}
                    for (en, sl), seq in l["regseq"].items():
                        if en != eng_name:
                            continue
                        reg = eng.alloc_register()
                        regs[sl] = (reg, seq)
                        eng.reg_mov(reg, seq[0][1])
                    with eng.Fori(0, l["n"]) as iv:
                        l["var"].val = iv
                        for j in body_ids:
                            st = self.steps[j]
                            for w in plan[j][1]:
                                if w[2] == "static":
                                    eng.wait_ge(sems[w[0]], w[1])
                                else:
                                    reg, seq = regs[w[0]]
                                    eng.wait_ge(sems[w[0]], reg)
                                    pos = seq.index((j, w[1]))
                                    if pos + 1 < len(seq):
                                        nxt = seq[pos + 1][1]
                                    else:
                                        nxt = seq[0][1] + l["delta"][w[0]]
                                    dlt = nxt - w[1]
                                    if dlt:
                                        eng.reg_add(reg, reg, dlt)
                            r = st["fn"](eng)
                            if r is None:
                                r = eng.nop()
                            r.then_inc(sems[self._slot(st)], st["inc"])
                    for sl2, (reg, seq) in regs.items():
                        eng.free_register(reg)
                i = l["end"]
        return sems, body
def ln_consts(nc):
    ones_col_f = nc.inline_tensor(np.ones((128, 1), np.float32), name="ones_col_f")
    ones_row_f = nc.inline_tensor(np.ones((1, 128), np.float32), name="ones_row_f")
    return ones_col_f, ones_row_f


def emit_layernorm(E, sb, ps, toks, s_chunks, out_chunks, dep_s, nc_consts):
    onescol_sb, onesrow_sb = nc_consts["onescol_sb"], nc_consts["onesrow_sb"]
    ps_sum, ps_sq = ps["stat1"], ps["stat2"]
    ps_a, ps_b = ps["bca"], ps["bcb"]
    sq = sb["lnsq"]
    st_sum = sb["lnsum"]
    st_sq = sb["lnsq1"]
    st_m = sb["lnm"]
    st_r = sb["lnr"]
    st_mr = sb.get("lnmr", st_m)

    deps = list(dep_s)
    mm_sum = []
    for c in range(NCHUNK):
        mm_sum.append(E.emit("tensor", (lambda c: lambda t: t.matmul(
            ps_sum[0:1, :toks], onescol_sb[:, 0:1], s_chunks[c],
            start=(c == 0), stop=(c == NCHUNK - 1)))(c), deps=deps))
    sq_steps = []
    mm_sq = []
    for c in range(NCHUNK):
        s1 = E.emit("scalar", (lambda c: lambda sc: sc.activation(
            sq[:, :toks], s_chunks[c], AF.Square))(c),
            deps=deps + (mm_sq[-1:] if mm_sq else []))
        m1 = E.emit("tensor", (lambda c: lambda t: t.matmul(
            ps_sq[0:1, :toks], onescol_sb[:, 0:1], sq[:, :toks],
            start=(c == 0), stop=(c == NCHUNK - 1)))(c), deps=[s1])
        sq_steps.append(s1)
        mm_sq.append(m1)
    i1 = E.emit("vector", lambda v: v.tensor_scalar(
        st_m[0:1, :toks], ps_sum[0:1, :toks], 1.0 / D_MODEL, None, ALU.mult),
        deps=[mm_sum[-1]])
    i2 = E.emit("vector", lambda v: v.tensor_scalar(
        st_sq[0:1, :toks], ps_sq[0:1, :toks], 1.0 / D_MODEL, 1e-5,
        ALU.mult, ALU.add), deps=[mm_sq[-1]])
    i3 = E.emit("vector", lambda v: v.tensor_mul(
        st_sum[0:1, :toks], st_m[0:1, :toks], st_m[0:1, :toks]), deps=[i1])
    i4 = E.emit("vector", lambda v: v.tensor_sub(
        st_sq[0:1, :toks], st_sq[0:1, :toks], st_sum[0:1, :toks]), deps=[i2, i3])
    i5 = E.emit("scalar", lambda sc: sc.activation(
        st_r[0:1, :toks], st_sq[0:1, :toks], AF.Sqrt), deps=[i4])
    i6 = E.emit("vector", lambda v: v.reciprocal(
        st_r[0:1, :toks], st_r[0:1, :toks]), deps=[i5])
    i7 = E.emit("vector", lambda v: v.tensor_mul(
        st_mr[0:1, :toks], st_m[0:1, :toks], st_r[0:1, :toks]), deps=[i6, i3])
    b1 = E.emit("tensor", lambda t: t.matmul(
        ps_a[:, :toks], onesrow_sb[0:1, :], st_r[0:1, :toks],
        start=True, stop=True), deps=[i6])
    b2 = E.emit("tensor", lambda t: t.matmul(
        ps_b[:, :toks], onesrow_sb[0:1, :], st_mr[0:1, :toks],
        start=True, stop=True), deps=[i7])
    outs = []
    for c in range(NCHUNK):
        o1 = E.emit("vector", (lambda c: lambda v: v.tensor_mul(
            out_chunks[c], s_chunks[c], ps_a[:, :toks]))(c), deps=[b1])
        o2 = E.emit("vector", (lambda c: lambda v: v.tensor_sub(
            out_chunks[c], out_chunks[c], ps_b[:, :toks]))(c), deps=[o1, b2])
        outs.append(o2)
    return outs


def build_ffn(nc, E, es, xin, xout, wg, wu, wd, sg, su, sd, nc_consts,
              NI=64, prefix="f0_", deps0=()):
    TT = 512
    NT = TOK // TT
    sb = {}
    def sbuf(name, shape, dtype):
        sb[name] = es.enter_context(nc.sbuf_tensor(prefix + name, shape, dtype))
        return sb[name]
    xf = sbuf("xf", [128, NCHUNK, TT], F32)
    xb = sbuf("xb", [128, NCHUNK, TT], BF16)
    wgb = sbuf("wgb", [128, NCHUNK, 128], BF16)
    wub = sbuf("wub", [128, NCHUNK, 128], BF16)
    wdb = sbuf("wdb", [128, NI, 128], BF16)
    gate = sbuf("gate", [128, TT], BF16)
    hbuf = sbuf("hbuf", [128, NI, TT], BF16)
    sgc = sbuf("sgc", [128, NI], F32)
    suc = sbuf("suc", [128, NI], F32)
    sdc = sbuf("sdc", [128, NCHUNK], F32)
    sres = sbuf("sres", [128, NCHUNK, TT], F32)
    sbuf("lnsq", [128, TT], F32)
    sbuf("lnsum", [1, TT], F32)
    sbuf("lnsq1", [1, TT], F32)
    sbuf("lnm", [1, TT], F32)
    sbuf("lnr", [1, TT], F32)
    sbuf("lnmr", [1, TT], F32)

    ps = {}
    def psum(name, shape):
        ps[name] = es.enter_context(nc.psum_tensor(prefix + name, shape, F32))
        return ps[name]
    ps_g = psum("g", [128, TT])
    ps_u = psum("u", [128, TT])
    ps_d = psum("d", [128, TT])
    psum("stat1", [1, TT])
    psum("stat2", [1, TT])
    psum("bca", [128, TT])
    psum("bcb", [128, TT])

    d_sg = E.dma("sync", sgc[:, :], sg, deps=deps0)
    d_su = E.dma("sync", suc[:, :], su, deps=deps0)
    d_sd = E.dma("sync", sdc[:, :], sd, deps=deps0)

    lv = E.begin_loop(NT)
    first = list(deps0) + [d_sg, d_su, d_sd]
    d_x = E.dma("sync", None, None, deps=first)
    E.steps.pop()

    def dma_x(eng):
        return eng.dma_start(
            out=xf[:, :, :],
            in_=xin[:, bass.ds(lv.val * TT, TT)].rearrange(
                "(c p) n -> p c n", p=128))
    d_x = E.emit("sync", dma_x, deps=first, inc=16)
    c_x = E.emit("vector", lambda v: v.tensor_copy(
        xb[:, :, :].rearrange("p c n -> p (c n)"),
        xf[:, :, :].rearrange("p c n -> p (c n)")), deps=[d_x])

    h_steps = []
    prev_g = []
    prev_h = []
    for i in range(NI):
        d_wg = E.dma("gpsimd", wgb[:, :, :],
                     wg[i],
                     deps=[(x, -1) for x in prev_g] if i == 0 else prev_g)
        d_wu = E.dma("gpsimd", wub[:, :, :],
                     wu[i],
                     deps=[(x, -1) for x in prev_h] if i == 0 else prev_h)
        mg = []
        for ki in range(NCHUNK):
            mg.append(E.emit("tensor", (lambda ki=ki: lambda t_: t_.matmul(
                ps_g[:, :], wgb[:, ki, :], xb[:, ki, :],
                start=(ki == 0), stop=(ki == NCHUNK - 1)))(),
                deps=[d_wg, c_x] + (prev_g if i > 0
                                    else [(x, -1) for x in prev_g])))
        sg_i = E.emit("scalar", (lambda i=i: lambda sc: sc.activation(
            gate[:, :], ps_g[:, :], AF.Silu, scale=sgc[:, i:i + 1]))(),
            deps=[mg[-1]])
        mu = []
        for ki in range(NCHUNK):
            mu.append(E.emit("tensor", (lambda ki=ki: lambda t_: t_.matmul(
                ps_u[:, :], wub[:, ki, :], xb[:, ki, :],
                start=(ki == 0), stop=(ki == NCHUNK - 1)))(),
                deps=[d_wu, c_x] + (prev_h[-1:] if i > 0
                                    else [(x, -1) for x in prev_h])))
        h_i = E.emit("vector", (lambda i=i: lambda v: v.scalar_tensor_tensor(
            hbuf[:, i, :], ps_u[:, :], suc[:, i:i + 1], gate[:, :],
            ALU.mult, ALU.mult))(), deps=[mu[-1], sg_i])
        prev_g = [h_i]
        prev_h = [h_i]
        h_steps.append(h_i)

    s_steps = []
    prev_wd = []
    prev_ev = []
    for m in range(NCHUNK):
        d_wd = E.dma("gpsimd", wdb[:, :, :],
                     wd[m],
                     deps=prev_wd if m > 0 else prev_h)
        md = []
        for i in range(NI):
            md.append(E.emit("tensor", (lambda i=i: lambda t_: t_.matmul(
                ps_d[:, :], wdb[:, i, :], hbuf[:, i, :],
                start=(i == 0), stop=(i == NI - 1)))(),
                deps=[d_wd, h_steps[i]] + prev_ev))
        ev = E.emit("vector", (lambda m=m: lambda v: v.scalar_tensor_tensor(
            sres[:, m, :], ps_d[:, :], sdc[:, m:m + 1], xf[:, m, :],
            ALU.mult, ALU.add))(), deps=[md[-1], d_x])
        prev_wd = [ev]
        prev_ev = [ev]
        s_steps.append(ev)

    s_chunks = [sres[:, c, :] for c in range(NCHUNK)]
    ln_outs = emit_layernorm(E, sb, ps, TT, s_chunks, s_chunks,
                             s_steps, nc_consts)

    def dma_out(eng):
        return eng.dma_start(
            out=xout[:, bass.ds(lv.val * TT, TT)].rearrange(
                "(c p) n -> p c n", p=128),
            in_=sres[:, :, :])
    d_out = E.emit("sync", dma_out, deps=ln_outs, inc=16)
    E.end_loop()
    E.steps[d_x]["deps"] += [(d_out, -1)] + [(x, -1) for x in ln_outs]
    first_dwg = next(i for i in range(E.loops[-1]["start"], E.loops[-1]["end"])
                     if E.steps[i]["engine"] == "gpsimd")
    E.steps[first_dwg]["deps"] += [(h_steps[-1], -1)]
    E.steps[first_dwg + 1]["deps"] += [(h_steps[-1], -1)]
    return [d_out] + ln_outs


def _tiles_lhsT(w_t, km, mm):
    K, M = w_t.shape
    assert K % km == 0 and M % mm == 0
    return np.ascontiguousarray(
        w_t.reshape(K // km, km, M // mm, mm).transpose(2, 1, 0, 3))


def prep_ffn0(inputs, b0, b1):
    wg = _tiles_lhsT(np.asarray(inputs["L0_ffn_w_gate"]), 128, 128)
    wu = _tiles_lhsT(np.asarray(inputs["L0_ffn_w_up"]), 128, 128)
    wd = _tiles_lhsT(np.asarray(inputs["L0_ffn_w_down"]), 128, 128)
    sg = np.ascontiguousarray(np.asarray(inputs["L0_ffn_s_gate"]).reshape(64, 128).T)
    su = np.ascontiguousarray(np.asarray(inputs["L0_ffn_s_up"]).reshape(64, 128).T)
    sd = np.ascontiguousarray(np.asarray(inputs["L0_ffn_s_down"]).reshape(16, 128).T)
    return dict(wg0=wg, wu0=wu, wd0=wd, sg0=sg, su0=su, sd0=sd)


def build_ffn0_program():
    nc = bass.Bass("TRN2", target_bir_lowering=False, debug=False)
    xin = nc.dram_tensor("xin", [D_MODEL, TOK], F32, kind="ExternalInput").ap()
    xout = nc.dram_tensor("xout", [D_MODEL, TOK], F32, kind="ExternalOutput").ap()
    wg = nc.dram_tensor("wg0", [64, 128, 16, 128], I8, kind="ExternalInput").ap()
    wu = nc.dram_tensor("wu0", [64, 128, 16, 128], I8, kind="ExternalInput").ap()
    wd = nc.dram_tensor("wd0", [16, 128, 64, 128], I8, kind="ExternalInput").ap()
    sg = nc.dram_tensor("sg0", [128, 64], F32, kind="ExternalInput").ap()
    su = nc.dram_tensor("su0", [128, 64], F32, kind="ExternalInput").ap()
    sd = nc.dram_tensor("sd0", [128, 16], F32, kind="ExternalInput").ap()
    onescol = nc.inline_tensor(np.ones((128, 1), np.float32), name="onescol_c")
    onesrow = nc.inline_tensor(np.ones((1, 128), np.float32), name="onesrow_c")
    with ExitStack() as es:
        onescol_sb = es.enter_context(nc.sbuf_tensor("onescol", [128, 1], F32))
        onesrow_sb = es.enter_context(nc.sbuf_tensor("onesrow", [1, 128], F32))
        nc_consts = dict(onescol_sb=onescol_sb, onesrow_sb=onesrow_sb)
        E = Emit(serial=False)
        dc1 = E.dma("sync", onescol_sb[:, :], onescol.ap())
        dc2 = E.dma("sync", onesrow_sb[0:1, :], onesrow.ap())
        fin = build_ffn(nc, E, es, xin, xout, wg, wu, wd, sg, su, sd, nc_consts)

        def fini(eng):
            return eng.nop()
        E.emit("sync", fini, deps=fin)
        sems, body = E.lower(nc, es)
        block = es.enter_context(nc.Block())
        block.tensor(lambda t: body("tensor", t))
        block.vector(lambda v: body("vector", v))
        block.scalar(lambda sc: body("scalar", sc))
        block.gpsimd(lambda g: body("gpsimd", g))
        block.sync(lambda s: body("sync", s))
    return nc


def prep_ffn1(inputs):
    gw = np.ascontiguousarray(np.asarray(inputs["L1_ffn_gate_w"]).T)
    uw = np.ascontiguousarray(np.asarray(inputs["L1_ffn_up_w"]).T)
    dw = np.ascontiguousarray(np.asarray(inputs["L1_ffn_down_w"]).T)
    wg = _tiles_lhsT(gw, 128, 128)
    wu = _tiles_lhsT(uw, 128, 128)
    wd = _tiles_lhsT(dw, 128, 128)
    sg = np.ascontiguousarray(np.asarray(inputs["L1_ffn_gate_s"]).reshape(44, 128).T)
    su = np.ascontiguousarray(np.asarray(inputs["L1_ffn_up_s"]).reshape(44, 128).T)
    sd = np.ascontiguousarray(np.asarray(inputs["L1_ffn_down_s"]).reshape(16, 128).T)
    return dict(wg1=wg, wu1=wu, wd1=wd, sg1=sg, su1=su, sd1=sd)


def build_ffn1_program():
    nc = bass.Bass("TRN2", target_bir_lowering=False, debug=False)
    xin = nc.dram_tensor("xin", [D_MODEL, TOK], F32, kind="ExternalInput").ap()
    xout = nc.dram_tensor("xout", [D_MODEL, TOK], F32, kind="ExternalOutput").ap()
    wg = nc.dram_tensor("wg1", [44, 128, 16, 128], F32, kind="ExternalInput").ap()
    wu = nc.dram_tensor("wu1", [44, 128, 16, 128], F32, kind="ExternalInput").ap()
    wd = nc.dram_tensor("wd1", [16, 128, 44, 128], F32, kind="ExternalInput").ap()
    sg = nc.dram_tensor("sg1", [128, 44], F32, kind="ExternalInput").ap()
    su = nc.dram_tensor("su1", [128, 44], F32, kind="ExternalInput").ap()
    sd = nc.dram_tensor("sd1", [128, 16], F32, kind="ExternalInput").ap()
    onescol = nc.inline_tensor(np.ones((128, 1), np.float32), name="onescol_c")
    onesrow = nc.inline_tensor(np.ones((1, 128), np.float32), name="onesrow_c")
    with ExitStack() as es:
        onescol_sb = es.enter_context(nc.sbuf_tensor("onescol", [128, 1], F32))
        onesrow_sb = es.enter_context(nc.sbuf_tensor("onesrow", [1, 128], F32))
        nc_consts = dict(onescol_sb=onescol_sb, onesrow_sb=onesrow_sb)
        E = Emit(serial=False)
        E.dma("sync", onescol_sb[:, :], onescol.ap())
        E.dma("sync", onesrow_sb[0:1, :], onesrow.ap())
        fin = build_ffn(nc, E, es, xin, xout, wg, wu, wd, sg, su, sd, nc_consts,
                        NI=44, prefix="f1_")
        def fini(eng):
            return eng.nop()
        E.emit("sync", fini, deps=fin)
        sems, body = E.lower(nc, es)
        block = es.enter_context(nc.Block())
        block.tensor(lambda t: body("tensor", t))
        block.vector(lambda v: body("vector", v))
        block.scalar(lambda sc: body("scalar", sc))
        block.gpsimd(lambda g: body("gpsimd", g))
        block.sync(lambda s: body("sync", s))
    return nc




RSQ = 1.0 / float(np.sqrt(128.0))


def _rope_tables():
    j = np.arange(0, 16, 2, dtype=np.float64) / 16.0
    inv_freq = 1.0 / np.power(10000.0, j)
    t = np.arange(SEQ, dtype=np.float64) / 40.0
    freqs = np.outer(t, inv_freq)
    cos = np.cos(freqs).T
    sin = np.sin(freqs).T
    cos32 = np.tile(cos, (4, 1)).astype(np.float32)
    sin32 = np.tile(sin, (4, 1)).astype(np.float32)
    return cos32, sin32


def _rot_perm():
    perm = []
    for c in range(4):
        for j in range(8):
            for i in range(4):
                pass
        blk = []
        for i in range(4):
            for j in range(8):
                blk.append((4 * c + i) * 32 + j)
        for i in range(4):
            for j in range(8, 16):
                blk.append((4 * c + i) * 32 + j)
        for i in range(4):
            for j in range(16, 32):
                blk.append((4 * c + i) * 32 + j)
        perm.extend(blk)
    return np.array(perm)


def _perm_consts():
    ps = []
    for i in range(4):
        P = np.zeros((128, 32), np.float32)
        for m in range(8):
            P[i * 8 + m, m] = 1
        for m in range(8, 16):
            P[32 + i * 8 + (m - 8), m] = 1
        for m in range(16, 32):
            P[64 + i * 16 + (m - 16), m] = 1
        ps.append(P.astype(BF))
    return ps


def prep_attn1(inputs):
    g = lambda k: np.asarray(inputs[k])
    perm = _rot_perm()
    out = {}
    out["wdkv"] = _tiles_lhsT(g("L1_attn_W_dkv"), 128, 128)
    out["wdq"] = _tiles_lhsT(g("L1_attn_W_dq"), 128, 128)
    out["bdkv"] = np.ascontiguousarray(g("L1_attn_b_dkv").reshape(4, 128).T)
    out["bdq"] = np.ascontiguousarray(g("L1_attn_b_dq").reshape(4, 128).T)
    wkr = g("L1_attn_W_kr")[:, perm]
    out["wkr"] = _tiles_lhsT(wkr, 128, 128)
    out["bkr"] = np.ascontiguousarray(g("L1_attn_b_kr")[perm].reshape(4, 128).T)
    wqr = g("L1_attn_W_qr")[:, perm]
    out["wqr"] = _tiles_lhsT(wqr, 128, 128)
    out["bqr"] = np.ascontiguousarray(g("L1_attn_b_qr")[perm].reshape(4, 128).T)
    wuk = g("L1_attn_W_uk")
    wk = np.zeros((16, 128, 4, 128), np.float32)
    wk[:, :, :, 0:96] = wuk.reshape(4, 128, 16, 96).transpose(2, 1, 0, 3)
    out["wuk"] = wk
    bk = np.zeros((128, 16), np.float32)
    bk[0:96, :] = g("L1_attn_b_uk").reshape(16, 96).T
    out["buk"] = bk
    wuq = g("L1_attn_W_uq")
    wq = np.zeros((16, 128, 4, 128), np.float32)
    wq[:, :, :, 0:96] = wuq.reshape(4, 128, 16, 96).transpose(2, 1, 0, 3)
    out["wuq"] = wq
    bq = np.zeros((128, 16), np.float32)
    bq[0:96, :] = g("L1_attn_b_uq").reshape(16, 96).T
    out["buq"] = bq
    wuv = g("L1_attn_W_uv")
    out["wuv"] = np.ascontiguousarray(
        wuv.reshape(4, 128, 16, 128).transpose(2, 1, 0, 3))
    out["buv"] = np.ascontiguousarray(g("L1_attn_b_uv").reshape(16, 128).T)
    out["wout1"] = _tiles_lhsT(g("L1_attn_W_out"), 128, 128)
    out["bout1"] = np.ascontiguousarray(g("L1_attn_b_out").reshape(16, 128).T)
    return out


def attn1_dram(nc):
    d = {}
    def inp(name, shape, dtype=F32):
        d[name] = nc.dram_tensor(name, list(shape), dtype, kind="ExternalInput").ap()
    inp("wdkv", [4, 128, 16, 128]); inp("wdq", [4, 128, 16, 128])
    inp("bdkv", [128, 4]); inp("bdq", [128, 4])
    inp("wkr", [4, 128, 16, 128]); inp("bkr", [128, 4])
    inp("wqr", [4, 128, 4, 128]); inp("bqr", [128, 4])
    inp("wuk", [16, 128, 4, 128]); inp("buk", [128, 16])
    inp("wuq", [16, 128, 4, 128]); inp("buq", [128, 16])
    inp("wuv", [16, 128, 4, 128]); inp("buv", [128, 16])
    inp("wout1", [16, 128, 16, 128]); inp("bout1", [128, 16])
    return d


def build_attn1(nc, E, es, xin, xout, dr, nc_consts, prefix="a1_", deps0=()):
    cos32, sin32 = _rope_tables()
    c_cos = nc.inline_tensor(cos32, name="c_cos")
    c_sin = nc.inline_tensor(sin32, name="c_sin")
    perms = _perm_consts()
    permpads = []
    for p in perms:
        pp = np.zeros((128, 128), np.float32)
        pp[:, 96:128] = p.astype(np.float32)
        permpads.append(pp.astype(BF))
    c_perm = [nc.inline_tensor(p, name=f"c_perm{i}") for i, p in enumerate(permpads)]
    tribig = np.triu(np.full((128, 128), -1e30, np.float32), 1)
    c_tribig = nc.inline_tensor(tribig, name="c_tribig")
    c_identb = nc.inline_tensor(np.eye(128, dtype=BF), name="c_identb")
    c_onesb = nc.inline_tensor(np.ones((1, 128), BF), name="c_onesb")
    ofm_d = nc.dram_tensor(prefix + "ofm_d", [2048, SEQ], BF16, kind="Internal").ap()

    sb = {}
    def sbuf(name, shape, dtype):
        sb[name] = es.enter_context(nc.sbuf_tensor(prefix + name, shape, dtype))
        return sb[name]
    cosb = sbuf("cosb", [32, SEQ], F32)
    sinb = sbuf("sinb", [32, SEQ], F32)
    permb = sbuf("permb", [128, 4, 128], BF16)
    tribigb = sbuf("tribigb", [128, 128], F32)
    identb = sbuf("identb", [128, 128], BF16)
    onesrow_b = sbuf("onesrow_b", [1, 128], BF16)
    w_proj = sbuf("w_proj", [128, NCHUNK, 128], BF16)
    w_ukh = sbuf("w_ukh", [128, 4, 128], BF16)
    w_uqh = sbuf("w_uqh", [128, 4, 128], BF16)
    w_uvh = sbuf("w_uvh", [128, 4, 128], BF16)
    w_outm = sbuf("w_outm", [128, NCHUNK, 128], BF16)
    b_dkv = sbuf("b_dkv", [128, 4], F32)
    b_dq = sbuf("b_dq", [128, 4], F32)
    b_kr = sbuf("b_kr", [128, 4], F32)
    b_qr = sbuf("b_qr", [128, 4], F32)
    b_uk = sbuf("b_uk", [128, 16], F32)
    b_uq = sbuf("b_uq", [128, 16], F32)
    b_uvb = sbuf("b_uvb", [128, 16], F32)
    v_fm = sbuf("v_fm", [128, SEQ], BF16)
    b_out = sbuf("b_out", [128, 16], F32)
    xb = sbuf("xb", [128, NCHUNK, 512], BF16)
    ckv = sbuf("ckv", [128, 4, SEQ], BF16)
    cq = sbuf("cq", [128, 4, SEQ], BF16)
    kro = sbuf("kro", [128, 4, SEQ], BF16)
    qro = sbuf("qro", [128, 4, SEQ], BF16)
    obuf = sbuf("obuf", [128, SEQ], BF16)
    ofmt = sbuf("ofmt", [128, NCHUNK, 512], BF16)
    rtmp = sbuf("rtmp", [128, 512], F32)
    rt1 = sbuf("rt1", [32, 512], F32)
    rt2 = sbuf("rt2", [32, 512], F32)
    rt3 = sbuf("rt3", [32, 512], F32)
    qb_h = sbuf("qb_h", [128, SEQ], BF16)
    kb_h = sbuf("kb_h", [128, SEQ], BF16)
    v_h = sbuf("v_h", [128, 16, 128], BF16)
    p_sb = sbuf("p_sb", [128, SEQ], BF16)
    pt_sb = sbuf("pt_sb", [128, 2, 512], BF16)
    negm = sbuf("negm", [128, 1], F32)
    lsum = sbuf("lsum", [128, 1], F32)
    rl = sbuf("rl", [128, 1], F32)
    xf = sbuf("xf", [128, NCHUNK, 512], F32)
    sbuf("lnr", [1, 512], F32)
    sb["lnsq"] = rtmp
    sb["lnsum"] = rt1
    sb["lnsq1"] = rt2
    sb["lnm"] = rt3

    ps = {}
    def psum(name, shape, dtype=F32):
        ps[name] = es.enter_context(nc.psum_tensor(prefix + name, shape, dtype))
        return ps[name]
    big = psum("big", [128, 2048])
    ps_pt = psum("pt", [128, 512], BF16)
    ps_o = psum("o", [128, 128])
    ps_p = psum("p", [128, 512])
    ps["d"] = big[:, 0:512]
    ps["stat1"] = big[0:1, 512:1024]
    ps["stat2"] = big[0:1, 1024:1536]
    ps["bca"] = big[:, 1024:1536]
    ps["bcb"] = big[:, 1536:2048]

    ld = list(deps0)
    ld.append(E.dma("sync", cosb[:, :], c_cos.ap(), deps=deps0))
    ld.append(E.dma("sync", sinb[:, :], c_sin.ap(), deps=deps0))
    for i in range(4):
        ld.append(E.dma("sync", permb[:, i, :], c_perm[i].ap()))
    ld.append(E.dma("sync", tribigb[:, :], c_tribig.ap()))
    ld.append(E.dma("sync", identb[:, :], c_identb.ap()))
    ld.append(E.dma("sync", onesrow_b[0:1, :], c_onesb.ap()))
    ld.append(E.dma("sync", b_dkv[:, :], dr["bdkv"]))
    ld.append(E.dma("sync", b_dq[:, :], dr["bdq"]))
    ld.append(E.dma("sync", b_kr[:, :], dr["bkr"]))
    ld.append(E.dma("sync", b_qr[:, :], dr["bqr"]))
    ld.append(E.dma("sync", b_uk[:, :], dr["buk"]))
    ld.append(E.dma("sync", b_uq[:, :], dr["buq"]))
    ld.append(E.dma("sync", b_uvb[:, :], dr["buv"], deps=deps0))
    ld.append(E.dma("sync", b_out[:, :], dr["bout1"]))

    fin_prev = list(deps0)
    for s in range(2):
        base = s * SEQ
        ckv_done = []
        rope_done = []
        prev_tile = list(fin_prev)
        prev_wp = []
        for tt in range(4):
            tsl = slice(base + tt * 512, base + (tt + 1) * 512)
            lsl = slice(tt * 512, (tt + 1) * 512)
            d_x = E.dma("gpsimd", xb[:, :, :],
                        xin[:, tsl].rearrange("(c p) n -> p c n", p=128),
                        deps=prev_tile + ld)
            tile_deps = []
            for which, wnm, bt, dst in (("kv", "wdkv", b_dkv, ckv),
                                        ("q", "wdq", b_dq, cq),
                                        ("kr", "wkr", b_kr, kro)):
                for m in range(4):
                    d_w = E.dma("gpsimd", w_proj[:, :, :],
                                dr[wnm][m],
                                deps=prev_wp)
                    mm = []
                    for ki in range(NCHUNK):
                        mm.append(E.emit("tensor", (lambda m=m, ki=ki: lambda t_: t_.matmul(
                            ps_p[:, :], w_proj[:, ki, :], xb[:, ki, :],
                            start=(ki == 0), stop=(ki == NCHUNK - 1)))(),
                            deps=[d_x, d_w] + tile_deps[-1:]))
                    prev_wp = [mm[-1]]
                    if which != "kr":
                        ev = E.emit("vector", (lambda bt=bt, dst=dst, m=m, lsl=lsl: lambda v: v.tensor_scalar(
                            dst[:, m, lsl], ps_p[:, :], bt[:, m:m + 1], None, ALU.add))(),
                            deps=[mm[-1]])
                        tile_deps = [ev]
                        ckv_done.append(ev)
                    else:
                        ev = E.emit("vector", (lambda bt=bt, m=m: lambda v: v.tensor_scalar(
                            rtmp[:, :], ps_p[:, :], bt[:, m:m + 1], None, ALU.add))(),
                            deps=[mm[-1]])
                        tile_deps = _emit_rope(E, rtmp, rt1, rt2, rt3, cosb, sinb,
                                               kro, m, lsl, ev, rope_done)
            for m in range(4):
                d_w = E.dma("gpsimd", w_proj[:, 0:4, :],
                            dr["wqr"][m],
                            deps=prev_wp)
                mm = []
                for ki in range(4):
                    mm.append(E.emit("tensor", (lambda m=m, ki=ki, lsl=lsl: lambda t_: t_.matmul(
                        ps_p[:, :], w_proj[:, ki, :], cq[:, ki, lsl],
                        start=(ki == 0), stop=(ki == 3)))(),
                        deps=[d_w] + tile_deps[-1:] + ckv_done[-1:]))
                prev_wp = [mm[-1]]
                ev = E.emit("vector", (lambda m=m: lambda v: v.tensor_scalar(
                    rtmp[:, :], ps_p[:, :], b_qr[:, m:m + 1], None,
                    ALU.add))(), deps=[mm[-1]])
                tile_deps = _emit_rope(E, rtmp, rt1, rt2, rt3, cosb, sinb,
                                       qro, m, lsl, ev, rope_done)
            prev_tile = tile_deps
        proj_done = ckv_done[-1:] + rope_done[-2:]

        head_done = list(proj_done)
        o_dmas = []
        for h in range(16):
            hp = []
            d_uq = E.dma("gpsimd", w_uqh[:, :, :],
                         dr["wuq"][h],
                         deps=head_done[-1:] + proj_done)
            d_uk = E.dma("gpsimd", w_ukh[:, :, :],
                         dr["wuk"][h],
                         deps=head_done[-1:] + proj_done)
            d_uv = E.dma("gpsimd", w_uvh[:, :, :],
                         dr["wuv"][h],
                         deps=head_done[-1:] + proj_done)
            for wt, bt, src, ro, dst, d_w, qsc in (
                    (w_uqh, b_uq, cq, qro, qb_h, d_uq, True),
                    (w_ukh, b_uk, ckv, kro, kb_h, d_uk, False)):
                for tt2 in range(4):
                    l2 = slice(tt2 * 512, (tt2 + 1) * 512)
                    mm = []
                    for ki in range(4):
                        mm.append(E.emit("tensor", (lambda wt=wt, ki=ki, src=src, l2=l2: lambda t_: t_.matmul(
                            ps_p[:, :], wt[:, ki, :], src[:, ki, l2],
                            start=(ki == 0), stop=False))(),
                            deps=[d_w] + head_done[-1:] + hp[-1:]))
                    mm.append(E.emit("tensor", (lambda ro=ro, h=h, l2=l2: lambda t_: t_.matmul(
                        ps_p[:, :], permb[:, h % 4, :], ro[:, h // 4, l2],
                        start=False, stop=True))(),
                        deps=rope_done[-1:]))
                    if qsc:
                        ev = E.emit("vector", (lambda bt=bt, h=h, dst=dst, l2=l2: lambda v: v.tensor_scalar(
                            dst[:, l2], ps_p[:, :], bt[:, h:h + 1], RSQ,
                            ALU.add, ALU.mult))(), deps=[mm[-1]])
                    else:
                        ev = E.emit("vector", (lambda bt=bt, h=h, dst=dst, l2=l2: lambda v: v.tensor_scalar(
                            dst[:, l2], ps_p[:, :], bt[:, h:h + 1], None,
                            ALU.add))(), deps=[mm[-1]])
                    hp = [ev]
            for tt2 in range(4):
                l2 = slice(tt2 * 512, (tt2 + 1) * 512)
                mm = []
                for ki in range(4):
                    mm.append(E.emit("tensor", (lambda ki=ki, l2=l2: lambda t_: t_.matmul(
                        ps_p[:, :], w_uvh[:, ki, :], ckv[:, ki, l2],
                        start=(ki == 0), stop=(ki == 3)))(),
                        deps=[d_uv] + hp[-1:]))
                ev = E.emit("vector", (lambda h=h, l2=l2: lambda v: v.tensor_scalar(
                    v_fm[:, l2], ps_p[:, :], b_uvb[:, h:h + 1], None, ALU.add))(),
                    deps=[mm[-1]])
                hp = [ev]
            for q4 in range(4):
                trs = []
                for j in range(4):
                    kb = 4 * q4 + j
                    trs.append(E.emit("tensor", (lambda kb=kb, j=j: lambda t_: t_.matmul(
                        ps_pt[:, j * 128:(j + 1) * 128],
                        v_fm[:, kb * 128:(kb + 1) * 128], identb[:, :],
                        is_transpose=True, start=True, stop=True))(),
                        deps=hp[-1:]))
                ev = E.emit("vector", (lambda q4=q4: lambda v: v.tensor_copy(
                    v_h[:, 4 * q4:4 * q4 + 4, :],
                    ps_pt[:, 0:512].rearrange("p (a b) -> p a b", a=4)))(),
                    deps=[trs[-1]])
                hp = [ev]
            qkv_done = list(hp)

            prev_ex = []
            prev_oev = []
            for qi in range(16):
                nkg = qi // 4 + 1
                klen = (qi + 1) * 128
                qsl = slice(qi * 128, (qi + 1) * 128)
                smm = []
                for kg in range(nkg):
                    kw = min(512, klen - kg * 512)
                    ksl = slice(kg * 512, kg * 512 + kw)
                    m1 = E.emit("tensor", (lambda qsl=qsl, ksl=ksl: lambda t_: t_.matmul(
                        big[:, ksl], qb_h[:, qsl], kb_h[:, ksl],
                        start=True, stop=True))(), deps=qkv_done + smm[-1:] + prev_ex)
                    smm.append(m1)
                dmask = E.emit("vector", (lambda qsl2=slice(qi * 128, klen): lambda v: v.tensor_add(
                    big[:, qsl2], big[:, qsl2], tribigb[:, :]))(), deps=[smm[-1]])
                mx = E.emit("vector", (lambda klen=klen: lambda v: v.reduce_max(
                    negm[:, 0:1], big[:, 0:klen], axis=mybir.AxisListType.X,
                    negate=True))(), deps=[dmask] + smm)
                ex = E.emit("scalar", (lambda klen=klen: lambda sc: sc.activation(
                    p_sb[:, 0:klen], big[:, 0:klen], AF.Exp, bias=negm[:, 0:1],
                    accum_out=lsum[:, 0:1]))(), deps=[mx])
                prev_ex = [ex]
                rc = E.emit("vector", lambda v: v.reciprocal(rl[:, 0:1], lsum[:, 0:1]),
                            deps=[ex])
                pscl = E.emit("gpsimd", (lambda klen=klen: lambda g: g.tensor_scalar(
                    p_sb[:, 0:klen], p_sb[:, 0:klen], rl[:, 0:1], None, ALU.mult))(),
                    deps=[rc])
                omm = []
                prev_tev = []
                nq4 = (qi + 1 + 3) // 4
                for g4 in range(nq4):
                    kb0 = 4 * g4
                    nkb = min(4, qi + 1 - kb0)
                    pb = g4 % 2
                    trs = []
                    for j in range(nkb):
                        kb = kb0 + j
                        trs.append(E.emit("tensor", (lambda kb=kb, j=j: lambda t_: t_.matmul(
                            ps_pt[:, j * 128:(j + 1) * 128],
                            p_sb[:, kb * 128:(kb + 1) * 128], identb[:, :],
                            is_transpose=True, start=True, stop=True))(),
                            deps=[pscl] + prev_tev))
                    tev = E.emit("vector", (lambda pb=pb, nkb=nkb: lambda v: v.tensor_copy(
                        pt_sb[:, pb, 0:nkb * 128], ps_pt[:, 0:nkb * 128]))(),
                        deps=[trs[-1]])
                    prev_tev = [tev]
                    for j in range(nkb):
                        kb = kb0 + j
                        omm.append(E.emit("tensor", (lambda kb=kb, j=j, pb=pb, qi=qi: lambda t_: t_.matmul(
                            ps_o[:, :], v_h[:, kb, :],
                            pt_sb[:, pb, j * 128:(j + 1) * 128],
                            start=(kb == 0), stop=(kb == qi)))(),
                            deps=[tev] + omm[-1:]))
                oev = E.emit("vector", (lambda qsl=qsl: lambda v: v.tensor_copy(
                    obuf[:, qsl], ps_o[:, :]))(), deps=[omm[-1]] + prev_oev)
                prev_oev = [oev]
                head_done = [oev]
            d_o = E.dma("sync", ofm_d[h * 128:(h + 1) * 128, :], obuf[:, :],
                        deps=[head_done[-1]])
            o_dmas.append(d_o)
            head_done = [d_o]
        attn_done = list(o_dmas)

        prev = list(attn_done)
        prev_wo = []
        for tt in range(4):
            tsl = slice(base + tt * 512, base + (tt + 1) * 512)
            lsl = slice(tt * 512, (tt + 1) * 512)
            d_xf = E.dma("sync", xf[:, :, :],
                         xin[:, tsl].rearrange("(c p) n -> p c n", p=128),
                         deps=prev)
            d_of = E.dma("sync", ofmt[:, :, :],
                         ofm_d[:, lsl].rearrange("(c p) n -> p c n", p=128),
                         deps=prev + attn_done)
            s_steps = []
            for m in range(NCHUNK):
                d_wo = E.dma("gpsimd", w_outm[:, :, :],
                             dr["wout1"][m],
                             deps=prev_wo)
                mm = []
                for ki in range(NCHUNK):
                    mm.append(E.emit("tensor", (lambda m=m, ki=ki: lambda t_: t_.matmul(
                        ps["d"], w_outm[:, ki, :], ofmt[:, ki, :],
                        start=(ki == 0), stop=(ki == NCHUNK - 1)))(),
                        deps=[d_xf, d_of, d_wo] + s_steps[-1:]))
                prev_wo = [mm[-1]]
                ev = E.emit("vector", (lambda m=m: lambda v: v.scalar_tensor_tensor(
                    xf[:, m, :], ps["d"], b_out[:, m:m + 1], xf[:, m, :],
                    ALU.add, ALU.add))(), deps=[mm[-1], d_xf])
                s_steps.append(ev)
            s_chunks = [xf[:, c, :] for c in range(NCHUNK)]
            ln_outs = emit_layernorm(E, sb, ps, 512, s_chunks, s_chunks,
                                     s_steps, nc_consts)
            d_out = E.dma("sync", xout[:, tsl].rearrange("(c p) n -> p c n", p=128),
                          xf[:, :, :], deps=ln_outs)
            prev = [d_out] + ln_outs
        fin_prev = prev
    return fin_prev


def _emit_rope(E, rtmp, rt1, rt2, rt3, cosb, sinb, dst, m, lsl, ev, rope_done):
    r0 = E.emit("vector", (lambda: lambda v: v.tensor_copy(
        rt3[:, :], rtmp[32:64, :]))(), deps=[ev])
    r1 = E.emit("vector", (lambda: lambda v: v.tensor_mul(
        rt1[:, :], rtmp[0:32, :], cosb[:, lsl]))(), deps=[ev])
    r2 = E.emit("vector", (lambda: lambda v: v.tensor_mul(
        rt2[:, :], rt3[:, :], sinb[:, lsl]))(), deps=[r0])
    r3 = E.emit("vector", (lambda: lambda v: v.tensor_sub(
        dst[0:32, m, lsl], rt1[:, :], rt2[:, :]))(), deps=[r1, r2])
    r4 = E.emit("vector", (lambda: lambda v: v.tensor_mul(
        rt1[:, :], rt3[:, :], cosb[:, lsl]))(), deps=[r3])
    r5 = E.emit("vector", (lambda: lambda v: v.tensor_mul(
        rt2[:, :], rtmp[0:32, :], sinb[:, lsl]))(), deps=[r3])
    r6 = E.emit("vector", (lambda: lambda v: v.tensor_add(
        dst[32:64, m, lsl], rt1[:, :], rt2[:, :]))(), deps=[r4, r5])
    r7 = E.emit("vector", (lambda: lambda v: v.tensor_copy(
        dst[64:128, m, lsl], rtmp[64:128, :]))(), deps=[ev, r4, r5])
    rope_done += [r6, r7]
    return [r6, r7]


def build_attn1_program():
    nc = bass.Bass("TRN2", target_bir_lowering=False, debug=False)
    xin = nc.dram_tensor("xin", [D_MODEL, TOK], F32, kind="ExternalInput").ap()
    xout = nc.dram_tensor("xout", [D_MODEL, TOK], F32, kind="ExternalOutput").ap()
    dr = attn1_dram(nc)
    onescol = nc.inline_tensor(np.ones((128, 1), np.float32), name="onescol_c")
    onesrow = nc.inline_tensor(np.ones((1, 128), np.float32), name="onesrow_c")
    with ExitStack() as es:
        onescol_sb = es.enter_context(nc.sbuf_tensor("onescol", [128, 1], F32))
        onesrow_sb = es.enter_context(nc.sbuf_tensor("onesrow", [1, 128], F32))
        nc_consts = dict(onescol_sb=onescol_sb, onesrow_sb=onesrow_sb)
        E = Emit(serial=False)
        E.dma("sync", onescol_sb[:, :], onescol.ap())
        E.dma("sync", onesrow_sb[0:1, :], onesrow.ap())
        fin = build_attn1(nc, E, es, xin, xout, dr, nc_consts)
        def fini(eng):
            return eng.nop()
        E.emit("sync", fini, deps=fin)
        sems, body = E.lower(nc, es)
        block = es.enter_context(nc.Block())
        block.tensor(lambda t: body("tensor", t))
        block.vector(lambda v: body("vector", v))
        block.scalar(lambda sc: body("scalar", sc))
        block.gpsimd(lambda g: body("gpsimd", g))
        block.sync(lambda s: body("sync", s))
    return nc




RMS_EPS = 1.1920929e-07
NH = 64
HD = 64
DST = 128
DIN = 4096
NXC = 32
NBC = 34
NPC = 35


def prep_mamba(inputs):
    g = lambda k: np.asarray(inputs[k])
    W_in = g("L0_attn_W_in")
    WT = np.ascontiguousarray(W_in.T)
    w_z = _tiles_lhsT(WT[:, :DIN], 128, 128)
    xbc = WT[:, DIN:]
    xbc_pad = np.zeros((2048, NPC * 128), np.float32)
    xbc_pad[:, :4416] = xbc
    w_xbc = _tiles_lhsT(xbc_pad, 128, 128)
    W_out = g("L0_attn_W_out")
    w_o = _tiles_lhsT(np.ascontiguousarray(W_out.T), 128, 128)
    cw = g("L0_attn_conv_w")
    cw_pad = np.zeros((NPC * 128, 4), np.float32)
    cw_pad[:4352] = cw
    convw = np.ascontiguousarray(cw_pad.reshape(NPC, 128, 4).transpose(1, 0, 2))
    cb = g("L0_attn_conv_b")
    cb_pad = np.zeros((NPC * 128,), np.float32)
    cb_pad[:4352] = cb
    convb = np.ascontiguousarray(cb_pad.reshape(NPC, 128).T)
    dtb = g("L0_attn_dt_bias").reshape(64, 1)
    alog = g("L0_attn_A_log").reshape(64, 1)
    D = g("L0_attn_D")
    drep = np.zeros((128, 32), np.float32)
    for c in range(32):
        drep[0:64, c] = D[2 * c]
        drep[64:128, c] = D[2 * c + 1]
    nw = np.ascontiguousarray(g("L0_attn_norm_w").reshape(32, 128).T)
    return dict(w_z=w_z, w_xbc=w_xbc, w_o=w_o, convw=convw, convb=convb,
                dtb=dtb, alog=alog, drep=drep, nw=nw)


def mamba_dram(nc):
    d = {}
    def inp(name, shape, dtype=F32):
        d[name] = nc.dram_tensor(name, list(shape), dtype, kind="ExternalInput").ap()
    inp("w_z", [32, 128, 16, 128]); inp("w_xbc", [NPC, 128, 16, 128])
    inp("w_o", [16, 128, 32, 128])
    inp("convw", [128, NPC, 4]); inp("convb", [128, NPC])
    inp("dtb", [64, 1]); inp("alog", [64, 1]); inp("drep", [128, 32])
    inp("nw", [128, 32])
    return d


def build_mamba(nc, E, es, xin, xout, dr, nc_consts, prefix="m0_", deps0=()):
    GT = M_GT
    NG = SEQ // GT
    CPG = GT // 128
    trif = np.triu(np.ones((128, 128), np.float32))
    c_trif = nc.inline_tensor(trif, name="m_trif")
    c_trinegf = nc.inline_tensor(-trif, name="m_trinegf")
    c_identf = nc.inline_tensor(np.eye(128, dtype=np.float32), name="m_identf")
    e127 = np.zeros((128, 128), np.float32); e127[127, :] = 1.0
    c_e127f = nc.inline_tensor(e127, name="m_e127f")
    c_identnegf = nc.inline_tensor(-np.eye(128, dtype=np.float32), name="m_identnegf")
    c_identb2 = nc.inline_tensor(np.eye(128, dtype=BF), name="m_identb2")

    sb = {}
    def sbuf(name, shape, dtype):
        sb[name] = es.enter_context(nc.sbuf_tensor(prefix + name, shape, dtype))
        return sb[name]
    trif_b = sbuf("trif", [128, 128], F32)
    trinegf_b = sbuf("trinegf", [128, 128], F32)
    identf_b = sbuf("identf", [128, 128], F32)
    e127_b = sbuf("e127", [128, 128], F32)
    identnegf_b = sbuf("identnegf", [128, 128], F32)
    identb_b = sbuf("identb", [128, 128], BF16)
    convw_b = sbuf("convw", [128, NPC, 4], F32)
    convb_b = sbuf("convb", [128, NPC], F32)
    dtb_b = sbuf("dtb", [64, 1], F32)
    alog_b = sbuf("alog", [64, 1], F32)
    acol = sbuf("acol", [64, 1], F32)
    drep_b = sbuf("drep", [128, 32], F32)
    nw_b = sbuf("nw", [128, 32], F32)
    wbuf = sbuf("wbuf", [128, 32, 128], BF16)
    uf = sbuf("uf", [128, NCHUNK, GT], F32)
    ub = uf[:, :, 0:GT // 2].bitcast(BF16)
    xbc = sbuf("xbc", [128, NBC, GT + 3], BF16)
    xcs = sbuf("xcs", [128, NBC, GT], BF16)
    cacc1 = sbuf("cacc1", [128, GT], BF16)
    cacc2 = sbuf("cacc2", [128, GT], BF16)
    dt_g = sbuf("dt_g", [64, GT], F32)
    g_g = sbuf("g_g", [64, GT], F32)
    lndt_g = sbuf("lndt_g", [64, GT], F32)
    ta = sbuf("ta", [64, GT], F32)
    hst = sbuf("hst", [128, DIN], F32)
    hbf = sbuf("hbf", [128, DIN], BF16)
    htmp = sbuf("htmp", [128, 128], F32)
    y_g = sbuf("y_g", [128, NXC, GT], BF16)
    x_tm = sbuf("x_tm", [128, NXC, 128], BF16)
    b_tm = sbuf("b_tm", [128, 128], BF16)
    gT = sbuf("gT", [128, 64], F32)
    lndtT = sbuf("lndtT", [128, 64], F32)
    acsT = sbuf("acsT", [128, 64], F32)
    eacsT = sbuf("eacsT", [128, 64], F32)
    decT = sbuf("decT", [128, 64], F32)
    cb_sb = sbuf("cb_sb", [128, 128], BF16)
    em_sb = sbuf("em_sb", [128, 512], F32)
    p_sb4 = sbuf("p_sb4", [128, 512], BF16)
    xt_sb = sbuf("xt_sb", [128, 128], BF16)
    yoff_sb = sbuf("yoff_sb", [128, 128], BF16)
    yofft_sb = sbuf("yofft_sb", [128, 128], BF16)
    gam_sb = sbuf("gam_sb", [128, 256], F32)
    zrow_b = sbuf("zrow_b", [1, 128], BF16)
    zrow_f = sbuf("zrow_f", [1, 128], F32)
    gtmp = sbuf("gtmp", [128, GT], BF16)
    ztmp = sbuf("ztmp", [128, GT], BF16)
    rms_r = sbuf("rms_r", [1, GT], F32)
    sb["lnsq"] = em_sb
    sb["lnsum"] = ta
    sb["lnsq1"] = lndt_g
    sb["lnm"] = dt_g
    sb["lnr"] = g_g

    ps = {}
    def psum(name, shape, dtype=F32):
        ps[name] = es.enter_context(nc.psum_tensor(prefix + name, shape, dtype))
        return ps[name]
    ps_proj = psum("proj", [128, GT])
    ps_dcb = psum("dcb", [128, 512])
    ps_s = psum("s", [128, 128])
    ps_y = psum("y", [128, 128])
    ps_u2 = psum("u2", [128, 128])
    ps_trb = psum("trb", [128, 512], BF16)
    ps_f = psum("f", [128, 512])
    ps_ln = psum("ln", [128, 512])
    ps["stat1"] = ps_ln[0:1, 0:512]
    ps["stat2"] = ps_f[0:1, 0:512]
    ps["bca"] = ps_ln[:, 0:512]
    ps["bcb"] = ps_dcb[:, 0:512]
    v_trf = ps_f[:, 0:128]
    v_acs = ps_f[:, 128:192]
    v_dec = ps_f[:, 192:256]
    v_gam = ps_f[:, 256:384]
    v_rms = ps_ln[:, 0:GT]

    ld = list(deps0)
    for t_, c_ in ((trif_b, c_trif), (trinegf_b, c_trinegf), (identf_b, c_identf),
                   (e127_b, c_e127f), (identnegf_b, c_identnegf),
                   (identb_b, c_identb2)):
        ld.append(E.dma("sync", t_[:, :], c_.ap(), deps=deps0))
    for nm, t_ in (("convw", convw_b), ("convb", convb_b), ("dtb", dtb_b),
                   ("alog", alog_b), ("drep", drep_b), ("nw", nw_b)):
        src_ap = dr[nm]
        if nm == "convw":
            src_ap = src_ap.rearrange("p c j -> p (c j)")
            ld.append(E.dma("sync", t_[:, :, :].rearrange("p c j -> p (c j)"), src_ap, deps=deps0))
        else:
            ld.append(E.dma("sync", t_[:, :] if t_.shape[1] > 1 else t_[:, 0:1], src_ap, deps=deps0))
    a1 = E.emit("scalar", lambda sc: sc.activation(acol[:, 0:1], alog_b[:, 0:1], AF.Exp),
                deps=ld)
    az = E.emit("vector", lambda v: v.memset(zrow_b[0:1, :], 0.0), deps=[])
    az2 = E.emit("vector", lambda v: v.memset(zrow_f[0:1, :], 0.0), deps=[])
    ld.append(az)
    ld.append(az2)
    a2 = E.emit("vector", lambda v: v.tensor_scalar(
        acol[:, 0:1], acol[:, 0:1], -1.0, None, ALU.mult), deps=[a1])
    ld.append(a2)

    fin_prev = list(deps0)
    for s in range(2):
        base = s * SEQ
        mz1 = E.emit("vector", lambda v: v.memset(hst[:, :], 0.0), deps=fin_prev + ld)
        mz2 = E.emit("vector", lambda v: v.memset(
            xbc[:, :, 0:3], 0.0), deps=fin_prev + ld)
        seq_prev = [mz1, mz2]
        if s == 0:
            prev_w = []
        lv = E.begin_loop(NG)
        for gi in range(1):
            def dma_u(eng, base=base, lv=lv):
                return eng.dma_start(
                    out=ub[:, :, :],
                    in_=xin[:, bass.ds(base + lv.val * GT, GT)].rearrange(
                        "(c p) n -> p c n", p=128))
            d_u = E.emit("gpsimd", dma_u, deps=seq_prev, inc=16)
            prev_ev = list(seq_prev)
            dt_done = []
            for m in range(NPC):
                d_w = E.dma("gpsimd", wbuf[:, 0:16, :],
                            dr["w_xbc"][m],
                            deps=prev_w)
                mm = []
                for ki in range(NCHUNK):
                    mm.append(E.emit("tensor", (lambda m=m, ki=ki: lambda t_: t_.matmul(
                        ps_proj[:, :], wbuf[:, ki, :], ub[:, ki, :],
                        start=(ki == 0), stop=(ki == NCHUNK - 1)))(),
                        deps=[d_u, d_w] + prev_ev[-1:]))
                prev_w = [mm[-1]]
                if m < NBC:
                    ev = E.emit("vector", (lambda m=m: lambda v: v.tensor_copy(
                        xbc[:, m, 3:3 + GT], ps_proj[:, :]))(), deps=[mm[-1]])
                    prev_ev = [ev]
                else:
                    e1 = E.emit("vector", lambda v: v.tensor_scalar(
                        dt_g[:, :], ps_proj[0:64, :], dtb_b[:, 0:1], 0.0,
                        ALU.add, ALU.max), deps=[mm[-1]])
                    if not (M_STAGES & 8):
                        prev_ev = [e1]
                        dt_done = [e1]
                        continue
                    e2 = E.emit("scalar", lambda sc: sc.activation(
                        ta[:, :], ps_proj[0:64, :], AF.Abs, bias=dtb_b[:, 0:1]),
                        deps=[mm[-1], e1])
                    e3 = E.emit("scalar", lambda sc: sc.activation(
                        ta[:, :], ta[:, :], AF.Exp, scale=-1.0), deps=[e2])
                    e4 = E.emit("scalar", lambda sc: sc.activation(
                        ta[:, :], ta[:, :], AF.Ln, bias=1.0), deps=[e3])
                    e5 = E.emit("vector", lambda v: v.tensor_add(
                        dt_g[:, :], dt_g[:, :], ta[:, :]), deps=[e1, e4])
                    e6 = E.emit("scalar", lambda sc: sc.activation(
                        lndt_g[:, :], dt_g[:, :], AF.Ln), deps=[e5])
                    e7 = E.emit("vector", lambda v: v.tensor_scalar(
                        g_g[:, :], dt_g[:, :], acol[:, 0:1], None, ALU.mult),
                        deps=[e5])
                    prev_ev = [e5]
                    dt_done = [e5, e6, e7]
            inproj_done = prev_ev + dt_done

            if not (M_STAGES & 1):
                seq_prev = inproj_done
                continue
            conv_done = []
            pc = []
            for c in range(NBC):
                t0 = E.emit("gpsimd", (lambda c=c: lambda g_: g_.tensor_scalar(
                    cacc1[:, :], xbc[:, c, 0:GT], convw_b[:, c, 0:1], None,
                    ALU.mult))(), deps=inproj_done + pc)
                t1 = E.emit("vector", (lambda c=c: lambda v: v.scalar_tensor_tensor(
                    cacc2[:, :], xbc[:, c, 1:1 + GT], convw_b[:, c, 1:2],
                    cacc1[:, :], ALU.mult, ALU.add))(), deps=[t0])
                t2 = E.emit("vector", (lambda c=c: lambda v: v.scalar_tensor_tensor(
                    cacc1[:, :], xbc[:, c, 2:2 + GT], convw_b[:, c, 2:3],
                    cacc2[:, :], ALU.mult, ALU.add))(), deps=[t1])
                t3 = E.emit("vector", (lambda c=c: lambda v: v.scalar_tensor_tensor(
                    cacc2[:, :], xbc[:, c, 3:3 + GT], convw_b[:, c, 3:4],
                    cacc1[:, :], ALU.mult, ALU.add))(), deps=[t2])
                sl = E.emit("scalar", (lambda c=c: lambda sc: sc.activation(
                    xcs[:, c, :], cacc2[:, :], AF.Silu, bias=convb_b[:, c:c + 1]))(),
                    deps=[t3])
                pc = [sl]
                conv_done.append(sl)
            hcp = E.emit("vector", lambda v: v.tensor_copy(
                xbc[:, :, 0:3], xbc[:, :, GT:GT + 3]),
                deps=conv_done)

            ssd_prev = list(conv_done) + dt_done
            if not (M_STAGES & 2):
                seq_prev = ssd_prev + [hcp]
                continue
            for ck in range(CPG):
                cs = slice(ck * 128, (ck + 1) * 128)
                tg = E.emit("tensor", (lambda cs=cs: lambda t_: t_.matmul(
                    v_trf[:, 0:64], g_g[0:64, cs], identf_b[0:64, 0:64],
                    is_transpose=True, start=True, stop=True))(), deps=ssd_prev)
                eg = E.emit("vector", lambda v: v.tensor_copy(gT[:, :], v_trf[:, 0:64]),
                            deps=[tg])
                tl = E.emit("tensor", (lambda cs=cs: lambda t_: t_.matmul(
                    v_trf[:, 0:64], lndt_g[0:64, cs], identf_b[0:64, 0:64],
                    is_transpose=True, start=True, stop=True))(), deps=[eg] + ssd_prev)
                el = E.emit("vector", lambda v: v.tensor_copy(lndtT[:, :], v_trf[:, 0:64]),
                            deps=[tl])
                xt_evs = []
                pe = [el]
                for q in range(NXC // 4):
                    txs = []
                    for j in range(4):
                        c = 4 * q + j
                        txs.append(E.emit("tensor", (lambda c=c, j=j, cs=cs: lambda t_: t_.matmul(
                            ps_trb[:, j * 128:(j + 1) * 128], xcs[:, c, cs],
                            identb_b[:, :],
                            is_transpose=True, start=True, stop=True))(),
                            deps=ssd_prev + pe[-1:]))
                    ex = E.emit("vector", (lambda q=q: lambda v: v.tensor_copy(
                        x_tm[:, 4 * q:4 * q + 4, :],
                        ps_trb[:, 0:512].rearrange("p (a b) -> p a b", a=4)))(),
                        deps=[txs[-1]])
                    pe = [ex]
                    xt_evs += [ex, ex, ex, ex]
                tb = E.emit("tensor", (lambda cs=cs: lambda t_: t_.matmul(
                    ps_trb[:, 0:128], xcs[:, 32, cs], identb_b[:, :],
                    is_transpose=True, start=True, stop=True))(), deps=pe)
                eb = E.emit("vector", lambda v: v.tensor_copy(b_tm[:, :], ps_trb[:, 0:128]),
                            deps=[tb])
                m_acs = E.emit("tensor", lambda t_: t_.matmul(
                    v_acs[:, 0:64], trif_b[:, :], gT[:, :], start=True, stop=True),
                    deps=[eg, el])
                e_acs = E.emit("vector", lambda v: v.tensor_copy(acsT[:, :], v_acs[:, 0:64]),
                               deps=[m_acs])
                e_eacs = E.emit("scalar", lambda sc: sc.activation(
                    eacsT[:, :], acsT[:, :], AF.Exp), deps=[e_acs])
                m_d1 = E.emit("tensor", lambda t_: t_.matmul(
                    v_dec[:, 0:64], e127_b[:, :], acsT[:, :], start=True, stop=False),
                    deps=[e_acs])
                m_d2 = E.emit("tensor", lambda t_: t_.matmul(
                    v_dec[:, 0:64], identnegf_b[:, :], acsT[:, :],
                    start=False, stop=False), deps=[m_d1])
                m_d3 = E.emit("tensor", lambda t_: t_.matmul(
                    v_dec[:, 0:64], identf_b[:, :], lndtT[:, :],
                    start=False, stop=True), deps=[m_d2, el])
                e_dec = E.emit("scalar", lambda sc: sc.activation(
                    decT[:, :], v_dec[:, 0:64], AF.Exp), deps=[m_d3])
                m_cb = E.emit("tensor", (lambda cs=cs: lambda t_: t_.matmul(
                    ps_dcb[:, 0:128], xcs[:, 32, cs], xcs[:, 33, cs],
                    start=True, stop=True))(), deps=ssd_prev + [eb])
                e_cb = E.emit("vector", lambda v: v.tensor_copy(cb_sb[:, :],
                                                               ps_dcb[:, 0:128]),
                              deps=[m_cb])
                e_hbf = E.emit("vector", lambda v: v.tensor_copy(hbf[:, :], hst[:, :]),
                               deps=ssd_prev + [eb])

                p_done = {}
                d_prev = [e_cb]
                for h4 in range(16):
                    h0 = h4 * 4
                    dmm = []
                    dmm.append(E.emit("tensor", (lambda h0=h0: lambda t_: t_.matmul(
                        ps_dcb[:, 0:512], trinegf_b[:, :],
                        gT[:, h0:h0 + 4].unsqueeze(2).broadcast_to([128, 4, 128]),
                        start=True, stop=False))(), deps=[eg] + d_prev))
                    for j in range(4):
                        h = h4 * 4 + j
                        jsl = slice(j * 128, (j + 1) * 128)
                        dmm.append(E.emit("tensor", (lambda h=h, jsl=jsl: lambda t_: t_.matmul(
                            ps_dcb[:, jsl], gT[:, h:h + 1].broadcast_to([128, 128]),
                            trif_b[:, :], start=False, stop=False))(),
                            deps=[]))
                    dmm.append(E.emit("tensor", (lambda h0=h0: lambda t_: t_.matmul(
                        ps_dcb[:, 0:512], identf_b[:, :],
                        lndtT[:, h0:h0 + 4].unsqueeze(2).broadcast_to([128, 4, 128]),
                        start=False, stop=True))(), deps=[el]))
                    e_x = E.emit("scalar", (lambda: lambda sc: sc.activation(
                        em_sb[:, :], ps_dcb[:, 0:512], AF.Exp))(), deps=[dmm[-1]])
                    msk = E.emit("gpsimd", (lambda: lambda g_: g_.affine_select(
                        em_sb[:, :], em_sb[:, :], [[0, 4], [1, 128]], ALU.is_ge,
                        0.0, base=0, channel_multiplier=-1))(), deps=[e_x])
                    pmu = E.emit("gpsimd", (lambda: lambda g_: g_.tensor_tensor(
                        p_sb4[:, :], em_sb[:, :],
                        cb_sb[:, :].unsqueeze(1).broadcast_to([128, 4, 128]),
                        ALU.mult))(), deps=[msk, e_cb])
                    mg_ = E.emit("tensor", (lambda h0=h0: lambda t_: t_.matmul(
                        ps_ln[:, 0:256], e127_b[:, :],
                        acsT[:, h0:h0 + 4].unsqueeze(2).broadcast_to([128, 4, 64]),
                        start=True, stop=True))(), deps=[e_acs] + d_prev)
                    gex = E.emit("scalar", (lambda: lambda sc: sc.activation(
                        gam_sb[:, :], ps_ln[:, 0:256], AF.Exp))(),
                        deps=[mg_] + d_prev)
                    d_prev = [pmu, gex]

                    for pj in range(2):
                        c = h4 * 2 + pj
                        he, ho = 4 * h4 + 2 * pj, 4 * h4 + 2 * pj + 1
                        pe_sl = slice((2 * pj) * 128, (2 * pj + 1) * 128)
                        po_sl = slice((2 * pj + 1) * 128, (2 * pj + 2) * 128)
                        mu_ = E.emit("tensor", (lambda c=c, cs=cs: lambda t_: t_.matmul(
                            ps_u2[:, :], xcs[:, 33, cs], hbf[:, c * 128:(c + 1) * 128],
                            start=True, stop=True))(),
                            deps=[e_hbf, pmu] + d_prev + ssd_prev)
                        sc1 = E.emit("scalar", (lambda he=he: lambda sc: sc.activation(
                            yoff_sb[:, 0:64], ps_u2[:, 0:64], AF.Copy,
                            scale=eacsT[:, he:he + 1]))(), deps=[mu_, e_eacs])
                        sc2 = E.emit("scalar", (lambda ho=ho: lambda sc: sc.activation(
                            yoff_sb[:, 64:128], ps_u2[:, 64:128], AF.Copy,
                            scale=eacsT[:, ho:ho + 1]))(), deps=[sc1])
                        tr2 = E.emit("tensor", lambda t_: t_.matmul(
                            ps_trb[:, 0:128], yoff_sb[:, :], identb_b[:, :],
                            is_transpose=True, start=True, stop=True),
                            deps=[sc2] + d_prev)
                        tev2 = E.emit("vector", lambda v: v.tensor_copy(
                            yofft_sb[:, :], ps_trb[:, 0:128]), deps=[tr2])
                        my1 = E.emit("tensor", (lambda c=c, pe_sl=pe_sl: lambda t_: t_.matmul(
                            ps_y[0:64, :], x_tm[:, c, 0:64], p_sb4[:, pe_sl],
                            start=True, stop=True))(),
                            deps=[pmu, xt_evs[c], tr2] + d_prev)
                        my2 = E.emit("tensor", (lambda c=c, po_sl=po_sl: lambda t_: t_.matmul(
                            ps_y[64:128, :], x_tm[:, c, 64:128], p_sb4[:, po_sl],
                            start=True, stop=True))(), deps=[my1])
                        yev = E.emit("vector", (lambda c=c, cs=cs, ck=ck: lambda v: v.scalar_tensor_tensor(
                            y_g[:, c, cs], xcs[:, c, cs], drep_b[:, c:c + 1],
                            ps_y[:, :], ALU.mult, ALU.add))(), deps=[my2, tev2])
                        yev2 = E.emit("vector", (lambda c=c, cs=cs: lambda v: v.tensor_add(
                            y_g[:, c, cs], y_g[:, c, cs], yofft_sb[:, :]))(),
                            deps=[yev])
                        xs1 = E.emit("gpsimd", (lambda c=c, he=he: lambda g_: g_.tensor_scalar(
                            xt_sb[:, 0:64], x_tm[:, c, 0:64], decT[:, he:he + 1],
                            None, ALU.mult))(), deps=[e_dec, xt_evs[c], yev2])
                        xs2 = E.emit("gpsimd", (lambda c=c, ho=ho: lambda g_: g_.tensor_scalar(
                            xt_sb[:, 64:128], x_tm[:, c, 64:128], decT[:, ho:ho + 1],
                            None, ALU.mult))(), deps=[xs1])
                        ms_ = E.emit("tensor", lambda t_: t_.matmul(
                            ps_s[:, :], b_tm[:, :], xt_sb[:, :],
                            start=True, stop=True), deps=[xs2, eb])
                        hu1 = E.emit("vector", (lambda c=c, pj=pj: lambda v: v.tensor_mul(
                            htmp[:, :], hst[:, c * 128:(c + 1) * 128],
                            gam_sb[:, pj * 128:(pj + 1) * 128]))(),
                            deps=[gex, e_hbf, ms_])
                        hu2 = E.emit("vector", (lambda c=c: lambda v: v.tensor_add(
                            hst[:, c * 128:(c + 1) * 128], htmp[:, :], ps_s[:, :]))(),
                            deps=[hu1, ms_])
                        d_prev = [hu2]
                ssd_prev = [d_prev[-1] if d_prev else None] + [yev2]
                ssd_prev = [x for x in ssd_prev if x is not None]

            if not (M_STAGES & 4):
                seq_prev = ssd_prev + [hcp]
                continue
            rms_mm = []
            sq_prev = []
            for c in range(NXC):
                s1 = E.emit("scalar", (lambda c=c: lambda sc: sc.activation(
                    em_sb[:, 0:GT], y_g[:, c, :], AF.Square))(),
                    deps=ssd_prev + sq_prev)
                m1 = E.emit("tensor", (lambda c=c: lambda t_: t_.matmul(
                    ps["stat1"][0:1, 0:GT // 2], nc_consts["onescol_sb"][:, 0:1],
                    em_sb[:, 0:GT // 2], start=(c == 0), stop=(c == NXC - 1)))(),
                    deps=[s1])
                m2 = E.emit("tensor", (lambda c=c: lambda t_: t_.matmul(
                    ps["stat2"][0:1, 0:GT // 2], nc_consts["onescol_sb"][:, 0:1],
                    em_sb[:, GT // 2:GT], start=(c == 0), stop=(c == NXC - 1)))(),
                    deps=[m1])
                sq_prev = [m2]
                rms_mm.append(m2)
            r1 = E.emit("vector", lambda v: v.tensor_scalar(
                rms_r[0:1, 0:GT // 2], ps["stat1"][0:1, 0:GT // 2], 1.0 / DIN,
                RMS_EPS, ALU.mult, ALU.add), deps=[rms_mm[-1]])
            r1b = E.emit("vector", lambda v: v.tensor_scalar(
                rms_r[0:1, GT // 2:GT], ps["stat2"][0:1, 0:GT // 2], 1.0 / DIN,
                RMS_EPS, ALU.mult, ALU.add), deps=[rms_mm[-1], r1])
            r2 = E.emit("scalar", lambda sc: sc.activation(
                rms_r[0:1, :], rms_r[0:1, :], AF.Sqrt), deps=[r1b])
            r3 = E.emit("vector", lambda v: v.reciprocal(
                rms_r[0:1, :], rms_r[0:1, :]), deps=[r2])
            rb = E.emit("tensor", lambda t_: t_.matmul(
                v_rms[:, :], nc_consts["onesrow_sb"][0:1, :], rms_r[0:1, :],
                start=True, stop=True), deps=[r3])
            g_done = []
            prev_gev = []
            for zm in range(NXC):
                d_w = E.dma("gpsimd", wbuf[:, 0:16, :],
                            dr["w_z"][zm], deps=prev_w)
                g1 = E.emit("vector", (lambda zm=zm: lambda v: v.tensor_mul(
                    gtmp[:, :], y_g[:, zm, :], v_rms[:, :]))(),
                    deps=[rb] + ssd_prev + prev_gev)
                mm = []
                for ki in range(NCHUNK):
                    mm.append(E.emit("tensor", (lambda zm=zm, ki=ki: lambda t_: t_.matmul(
                        ps_proj[:, :], wbuf[:, ki, :], ub[:, ki, :],
                        start=(ki == 0), stop=(ki == NCHUNK - 1)))(),
                        deps=[d_w, d_u] + prev_gev + [rb]))
                prev_w = [mm[-1]]
                sz = E.emit("scalar", (lambda zm=zm: lambda sc: sc.activation(
                    ztmp[:, :], ps_proj[:, :], AF.Silu))(), deps=[mm[-1]])
                gev = E.emit("vector", (lambda zm=zm: lambda v: v.scalar_tensor_tensor(
                    y_g[:, zm, :], ztmp[:, :], nw_b[:, zm:zm + 1], gtmp[:, :],
                    ALU.mult, ALU.mult))(), deps=[sz, g1])
                prev_gev = [gev]
                g_done.append(gev)
            def dma_uf(eng, base=base, lv=lv):
                return eng.dma_start(
                    out=uf[:, :, :],
                    in_=xin[:, bass.ds(base + lv.val * GT, GT)].rearrange(
                        "(c p) n -> p c n", p=128))
            d_uf = E.emit("sync", dma_uf, deps=g_done, inc=16)
            s_steps = []
            prev_wo = list(prev_w)
            for m in range(NCHUNK):
                d_w = E.dma("gpsimd", wbuf[:, :, :],
                            dr["w_o"][m],
                            deps=prev_wo)
                mm = []
                for ki in range(NXC):
                    mm.append(E.emit("tensor", (lambda m=m, ki=ki: lambda t_: t_.matmul(
                        ps_proj[:, 0:GT], wbuf[:, ki, :], y_g[:, ki, :],
                        start=(ki == 0), stop=(ki == NXC - 1)))(),
                        deps=[d_w] + g_done + s_steps[-1:]))
                prev_wo = [mm[-1]]
                prev_w = [mm[-1]]
                ev = E.emit("vector", (lambda m=m: lambda v: v.tensor_add(
                    uf[:, m, :], uf[:, m, :], ps_proj[:, 0:GT]))(),
                    deps=[mm[-1], d_uf])
                s_steps.append(ev)
            s_chunks = [uf[:, c, :] for c in range(NCHUNK)]
            ln_outs = emit_layernorm(E, sb, ps, GT, s_chunks, s_chunks,
                                     s_steps, nc_consts)
            def dma_out2(eng, base=base, lv=lv):
                return eng.dma_start(
                    out=xout[:, bass.ds(base + lv.val * GT, GT)].rearrange(
                        "(c p) n -> p c n", p=128),
                    in_=uf[:, :, :])
            d_out = E.emit("sync", dma_out2, deps=ln_outs, inc=16)
            seq_prev = [d_out] + ln_outs + [hcp]
        E.end_loop()
        E.steps[d_u]["deps"] += [(x, -1) for x in seq_prev]
        lp = E.loops[-1]
        first_dw = next(i for i in range(lp["start"], lp["end"])
                        if E.steps[i]["engine"] == "gpsimd" and i > d_u)
        E.steps[first_dw]["deps"] += [(prev_w[0], -1)]
        seq_prev = [(x, 0) for x in seq_prev]
        seq_prev = [x[0] for x in seq_prev]
        fin_prev = seq_prev
    return fin_prev


def build_mamba_program():
    nc = bass.Bass("TRN2", target_bir_lowering=False, debug=False)
    xin = nc.dram_tensor("xin", [D_MODEL, TOK], F32, kind="ExternalInput").ap()
    xout = nc.dram_tensor("xout", [D_MODEL, TOK], F32, kind="ExternalOutput").ap()
    dr = mamba_dram(nc)
    onescol = nc.inline_tensor(np.ones((128, 1), np.float32), name="onescol_c")
    onesrow = nc.inline_tensor(np.ones((1, 128), np.float32), name="onesrow_c")
    with ExitStack() as es:
        onescol_sb = es.enter_context(nc.sbuf_tensor("onescol", [128, 1], F32))
        onesrow_sb = es.enter_context(nc.sbuf_tensor("onesrow", [1, 128], F32))
        nc_consts = dict(onescol_sb=onescol_sb, onesrow_sb=onesrow_sb)
        E = Emit(serial=False)
        E.dma("sync", onescol_sb[:, :], onescol.ap())
        E.dma("sync", onesrow_sb[0:1, :], onesrow.ap())
        fin = build_mamba(nc, E, es, xin, xout, dr, nc_consts)
        def fini(eng):
            return eng.nop()
        E.emit("sync", fini, deps=fin)
        sems, body = E.lower(nc, es)
        block = es.enter_context(nc.Block())
        block.tensor(lambda t: body("tensor", t))
        block.vector(lambda v: body("vector", v))
        block.scalar(lambda sc: body("scalar", sc))
        block.gpsimd(lambda g: body("gpsimd", g))
        block.sync(lambda s: body("sync", s))
    return nc


def build_fused_program():
    nc = bass.Bass("TRN2", target_bir_lowering=False, debug=False)
    xin = nc.dram_tensor("xin", [D_MODEL, TOK], F32, kind="ExternalInput").ap()
    xout = nc.dram_tensor("xout", [D_MODEL, TOK], F32, kind="ExternalOutput").ap()
    x1 = nc.dram_tensor("x1", [D_MODEL, TOK], F32, kind="Internal").ap()
    x2 = nc.dram_tensor("x2", [D_MODEL, TOK], F32, kind="Internal").ap()
    x3 = nc.dram_tensor("x3", [D_MODEL, TOK], F32, kind="Internal").ap()
    drA = mamba_dram(nc)
    drC = attn1_dram(nc)
    d = {}
    def inp(name, shape, dtype=F32):
        d[name] = nc.dram_tensor(name, list(shape), dtype, kind="ExternalInput").ap()
    inp("wg0", [64, 128, 16, 128], I8); inp("wu0", [64, 128, 16, 128], I8)
    inp("wd0", [16, 128, 64, 128], I8)
    inp("sg0", [128, 64]); inp("su0", [128, 64]); inp("sd0", [128, 16])
    inp("wg1", [44, 128, 16, 128]); inp("wu1", [44, 128, 16, 128])
    inp("wd1", [16, 128, 44, 128])
    inp("sg1", [128, 44]); inp("su1", [128, 44]); inp("sd1", [128, 16])
    onescol = nc.inline_tensor(np.ones((128, 1), np.float32), name="onescol_c")
    onesrow = nc.inline_tensor(np.ones((1, 128), np.float32), name="onesrow_c")
    with ExitStack() as es:
        onescol_sb = es.enter_context(nc.sbuf_tensor("onescol", [128, 1], F32))
        onesrow_sb = es.enter_context(nc.sbuf_tensor("onesrow", [1, 128], F32))
        nc_consts = dict(onescol_sb=onescol_sb, onesrow_sb=onesrow_sb)
        E = Emit(serial=False)
        c1 = E.dma("sync", onescol_sb[:, :], onescol.ap())
        c2 = E.dma("sync", onesrow_sb[0:1, :], onesrow.ap())
        with ExitStack() as esA:
            build_mamba(nc, E, esA, xin, x1, drA, nc_consts, deps0=[c1, c2])
            t1 = E.tail()
        with ExitStack() as esB:
            build_ffn(nc, E, esB, x1, x2, d["wg0"], d["wu0"], d["wd0"],
                      d["sg0"], d["su0"], d["sd0"], nc_consts,
                      NI=64, prefix="f0_", deps0=t1)
            t2 = E.tail()
        with ExitStack() as esC:
            build_attn1(nc, E, esC, x2, x3, drC, nc_consts, deps0=t2)
            t3 = E.tail()
        with ExitStack() as esD:
            build_ffn(nc, E, esD, x3, xout, d["wg1"], d["wu1"], d["wd1"],
                      d["sg1"], d["su1"], d["sd1"], nc_consts,
                      NI=44, prefix="f1_", deps0=t3)
            t4 = E.tail()
        def fini(eng):
            return eng.nop()
        E.emit("sync", fini, deps=t4)
        sems, body = E.lower(nc, es)
        block = es.enter_context(nc.Block())
        block.tensor(lambda t: body("tensor", t))
        block.vector(lambda v: body("vector", v))
        block.scalar(lambda sc: body("scalar", sc))
        block.gpsimd(lambda g: body("gpsimd", g))
        block.sync(lambda s: body("sync", s))
    return nc


_PROGS = {}


def _get_prog(name):
    if name not in _PROGS:
        _PROGS[name] = {"mamba": build_mamba_program,
                        "ffn0": build_ffn0_program,
                        "attn1": build_attn1_program,
                        "ffn1": build_ffn1_program,
                        "fused": build_fused_program}[name]()
    return _PROGS[name]


def kernel(**inputs):
    n = 8
    u = np.asarray(inputs["L0_attn_u"])
    xins = []
    for c in range(n):
        xins.append(np.ascontiguousarray(
            np.concatenate([u[2 * c].T, u[2 * c + 1].T], axis=1)))

    wmap = {}
    wmap.update(prep_mamba(inputs))
    wmap.update(prep_ffn0(inputs, 0, 0))
    wmap.update(prep_attn1(inputs))
    wmap.update(prep_ffn1(inputs))

    nc = _get_prog("fused")
    in_maps = [dict(xin=xins[c], **wmap) for c in range(n)]
    res = run_bass_kernel_spmd(nc, in_maps, core_ids=list(range(n)))
    xs = [res.results[c]["xout"] for c in range(n)]

    out = np.empty((16, SEQ, D_MODEL), np.float32)
    for c in range(n):
        out[2 * c] = xs[c][:, 0:SEQ].T
        out[2 * c + 1] = xs[c][:, SEQ:].T
    return out


# version 08 (final)
# speedup vs baseline: 1.1128x; 1.1128x over previous
import numpy as np
import ml_dtypes
import concourse.bass as bass
import concourse.mybir as mybir
from concourse.bass_utils import run_bass_kernel_spmd
from contextlib import ExitStack

F32 = mybir.dt.float32
BF16 = mybir.dt.bfloat16
I8 = mybir.dt.int8
AF = mybir.ActivationFunctionType
ALU = mybir.AluOpType
BF = ml_dtypes.bfloat16

D_MODEL = 2048
NCHUNK = D_MODEL // 128
TOK = 4096
SEQ = 2048
M_GT = 512
M_STAGES = 15
M_DTN = 99


class Emit:
    ENGINES = ("tensor", "vector", "scalar", "gpsimd", "sync")

    class LoopVar:
        def __init__(self):
            self.val = 0

    NRING = 4

    def __init__(self, serial=False):
        self.steps = []
        self.serial = serial
        self.loops = []
        self._loop = None
        self._ring_ctr = {}
        self._ring_last = {}

    def emit(self, engine, fn, deps=(), inc=None):
        if inc is None:
            inc = 16 if fn.__name__ == "dma" else 1
        i = len(self.steps)
        deps = list(deps)
        if self.serial and i > 0:
            deps.append(i - 1)
        st = dict(engine=engine, fn=fn, inc=inc, deps=deps)
        if inc == 16:
            r = self._ring_ctr.get(engine, 0)
            self._ring_ctr[engine] = (r + 1) % self.NRING
            st["ring"] = r
            prev = self._ring_last.get((engine, r))
            if prev is not None:
                deps.append(prev)
            self._ring_last[(engine, r)] = i
        self.steps.append(st)
        return i

    def begin_loop(self, n):
        assert self._loop is None
        lv = Emit.LoopVar()
        self._loop = dict(start=len(self.steps), n=n, var=lv)
        return lv

    def end_loop(self):
        l = self._loop
        l["end"] = len(self.steps)
        self.loops.append(l)
        self._loop = None
        ring_first = {}
        ring_last = {}
        for i in range(l["start"], l["end"]):
            st = self.steps[i]
            if st["inc"] == 16:
                key = (st["engine"], st["ring"])
                ring_first.setdefault(key, i)
                ring_last[key] = i
        for key, fi in ring_first.items():
            self.steps[fi]["deps"].append((ring_last[key], -1))

    def tail(self):
        last = {}
        for i, st in enumerate(self.steps):
            last[self._slot(st)] = i
        return list(last.values())

    def dma(self, engine, out, in_, deps=()):
        def dma(eng):
            return eng.dma_start(out=out, in_=in_)
        return self.emit(engine, dma, deps, inc=16)

    @staticmethod
    def _slot(st):
        if st["inc"] == 16:
            return st["engine"] + "_dma%d" % st["ring"]
        return st["engine"]

    def lower(self, nc, es, prefix=""):
        slots = sorted({self._slot(st) for st in self.steps})
        sems = {sl: es.enter_context(nc.semaphore(prefix + "sem_" + sl))
                for sl in slots}
        in_loop = {}
        for l in self.loops:
            for i in range(l["start"], l["end"]):
                in_loop[i] = l
        for l in self.loops:
            d = {sl: 0 for sl in slots}
            for i in range(l["start"], l["end"]):
                d[self._slot(self.steps[i])] += self.steps[i]["inc"]
            l["delta"] = d
        cum = {}
        run = {sl: 0 for sl in slots}
        for i, st in enumerate(self.steps):
            sl = self._slot(st)
            run[sl] += st["inc"]
            cum[i] = run[sl]
            l = in_loop.get(i)
            if l is not None and i == l["end"] - 1:
                for sl2 in slots:
                    run[sl2] += l["delta"][sl2] * (l["n"] - 1)

        def dep_id(d):
            return d[0] if isinstance(d, tuple) else d

        def dep_off(d):
            return d[1] if isinstance(d, tuple) else 0

        step_at = {}
        for i, st in enumerate(self.steps):
            step_at[(self._slot(st), cum[i])] = i

        know_eng = {e: {sl: 0 for sl in slots} for e in self.ENGINES}
        know_after = [None] * len(self.steps)
        plan = [None] * len(self.steps)
        for i, st in enumerate(self.steps):
            e = st["engine"]
            l = in_loop.get(i)
            waits = []
            if l is None:
                ke = know_eng[e]
                for d in st["deps"]:
                    di = dep_id(d)
                    ds_ = self._slot(self.steps[di])
                    de = self.steps[di]["engine"]
                    if (de == e and self.steps[di]["inc"] == 1
                            and e == "tensor"):
                        continue
                    dl = in_loop.get(di)
                    if dl is not None:
                        v = cum[di] + dl["delta"][ds_] * (dl["n"] - 1)
                    else:
                        v = cum[di]
                    if ke.get(ds_, 0) < v:
                        waits.append((ds_, v))
                ded = {}
                for ds_, v in waits:
                    ded[ds_] = max(ded.get(ds_, 0), v)
                waits = sorted(ded.items())
                for ds_, v in waits:
                    ke[ds_] = max(ke[ds_], v)
                    j = step_at.get((ds_, v))
                    if j is not None and know_after[j] is not None:
                        for s2, v2 in know_after[j].items():
                            if ke[s2] < v2:
                                ke[s2] = v2
                plan[i] = ("plain", waits)
                ka = dict(ke)
                sl = self._slot(st)
                ka[sl] = max(ka[sl], cum[i])
                know_after[i] = ka
            else:
                if i == l["start"]:
                    l["_ke"] = {en: {sl: 0 for sl in slots}
                                for en in self.ENGINES}
                ke2 = l["_ke"][e]
                lw = []
                for d in st["deps"]:
                    di = dep_id(d)
                    off = dep_off(d)
                    ds_ = self._slot(self.steps[di])
                    de = self.steps[di]["engine"]
                    if (de == e and self.steps[di]["inc"] == 1
                            and e == "tensor"):
                        continue
                    if in_loop.get(di) is l:
                        v = cum[di] + off * l["delta"][ds_]
                        if ke2.get(ds_, 0) < v:
                            lw.append((ds_, v, "reg"))
                            ke2[ds_] = v
                    else:
                        v = cum[di]
                        if know_eng[e].get(ds_, 0) < v:
                            lw.append((ds_, v, "static"))
                            know_eng[e][ds_] = v
                plan[i] = ("loop", lw)
                know_after[i] = dict(know_eng[e])
            if l is not None and i == l["end"] - 1:
                for en in self.ENGINES:
                    know_eng[en] = {sl: 0 for sl in slots}

        for l in self.loops:
            l["regseq"] = {}
            for i in range(l["start"], l["end"]):
                st = self.steps[i]
                for w in plan[i][1]:
                    if w[2] == "reg":
                        l["regseq"].setdefault((st["engine"], w[0]), []).append(
                            (i, w[1]))

        def body(eng_name, eng):
            i = 0
            n_steps = len(self.steps)
            while i < n_steps:
                l = in_loop.get(i)
                if l is None:
                    st = self.steps[i]
                    if st["engine"] == eng_name:
                        for ds_, v in plan[i][1]:
                            eng.wait_ge(sems[ds_], v)
                        r = st["fn"](eng)
                        if r is None:
                            r = eng.nop()
                        r.then_inc(sems[self._slot(st)], st["inc"])
                    i += 1
                    continue
                body_ids = [j for j in range(l["start"], l["end"])
                            if self.steps[j]["engine"] == eng_name]
                if body_ids:
                    regs = {}
                    for (en, sl), seq in l["regseq"].items():
                        if en != eng_name:
                            continue
                        reg = eng.alloc_register()
                        regs[sl] = (reg, seq)
                        eng.reg_mov(reg, seq[0][1])
                    with eng.Fori(0, l["n"]) as iv:
                        l["var"].val = iv
                        for j in body_ids:
                            st = self.steps[j]
                            for w in plan[j][1]:
                                if w[2] == "static":
                                    eng.wait_ge(sems[w[0]], w[1])
                                else:
                                    reg, seq = regs[w[0]]
                                    eng.wait_ge(sems[w[0]], reg)
                                    pos = seq.index((j, w[1]))
                                    if pos + 1 < len(seq):
                                        nxt = seq[pos + 1][1]
                                    else:
                                        nxt = seq[0][1] + l["delta"][w[0]]
                                    dlt = nxt - w[1]
                                    if dlt:
                                        eng.reg_add(reg, reg, dlt)
                            r = st["fn"](eng)
                            if r is None:
                                r = eng.nop()
                            r.then_inc(sems[self._slot(st)], st["inc"])
                    for sl2, (reg, seq) in regs.items():
                        eng.free_register(reg)
                i = l["end"]
        return sems, body
def ln_consts(nc):
    ones_col_f = nc.inline_tensor(np.ones((128, 1), np.float32), name="ones_col_f")
    ones_row_f = nc.inline_tensor(np.ones((1, 128), np.float32), name="ones_row_f")
    return ones_col_f, ones_row_f


def emit_layernorm(E, sb, ps, toks, s_chunks, out_chunks, dep_s, nc_consts):
    onescol_sb, onesrow_sb = nc_consts["onescol_sb"], nc_consts["onesrow_sb"]
    ps_sum, ps_sq = ps["stat1"], ps["stat2"]
    ps_a, ps_b = ps["bca"], ps["bcb"]
    sq = sb["lnsq"]
    st_sum = sb["lnsum"]
    st_sq = sb["lnsq1"]
    st_m = sb["lnm"]
    st_r = sb["lnr"]
    st_mr = sb.get("lnmr", st_m)

    deps = list(dep_s)
    mm_sum = []
    for c in range(NCHUNK):
        mm_sum.append(E.emit("tensor", (lambda c: lambda t: t.matmul(
            ps_sum[0:1, :toks], onescol_sb[:, 0:1], s_chunks[c],
            start=(c == 0), stop=(c == NCHUNK - 1)))(c), deps=deps))
    sq_steps = []
    mm_sq = []
    for c in range(NCHUNK):
        s1 = E.emit("scalar", (lambda c: lambda sc: sc.activation(
            sq[:, :toks], s_chunks[c], AF.Square))(c),
            deps=deps + (mm_sq[-1:] if mm_sq else []))
        m1 = E.emit("tensor", (lambda c: lambda t: t.matmul(
            ps_sq[0:1, :toks], onescol_sb[:, 0:1], sq[:, :toks],
            start=(c == 0), stop=(c == NCHUNK - 1)))(c), deps=[s1])
        sq_steps.append(s1)
        mm_sq.append(m1)
    i1 = E.emit("vector", lambda v: v.tensor_scalar(
        st_m[0:1, :toks], ps_sum[0:1, :toks], 1.0 / D_MODEL, None, ALU.mult),
        deps=[mm_sum[-1]])
    i2 = E.emit("vector", lambda v: v.tensor_scalar(
        st_sq[0:1, :toks], ps_sq[0:1, :toks], 1.0 / D_MODEL, 1e-5,
        ALU.mult, ALU.add), deps=[mm_sq[-1]])
    i3 = E.emit("vector", lambda v: v.tensor_mul(
        st_sum[0:1, :toks], st_m[0:1, :toks], st_m[0:1, :toks]), deps=[i1])
    i4 = E.emit("vector", lambda v: v.tensor_sub(
        st_sq[0:1, :toks], st_sq[0:1, :toks], st_sum[0:1, :toks]), deps=[i2, i3])
    i5 = E.emit("scalar", lambda sc: sc.activation(
        st_r[0:1, :toks], st_sq[0:1, :toks], AF.Sqrt), deps=[i4])
    i6 = E.emit("vector", lambda v: v.reciprocal(
        st_r[0:1, :toks], st_r[0:1, :toks]), deps=[i5])
    i7 = E.emit("vector", lambda v: v.tensor_mul(
        st_mr[0:1, :toks], st_m[0:1, :toks], st_r[0:1, :toks]), deps=[i6, i3])
    b1 = E.emit("tensor", lambda t: t.matmul(
        ps_a[:, :toks], onesrow_sb[0:1, :], st_r[0:1, :toks],
        start=True, stop=True), deps=[i6])
    b2 = E.emit("tensor", lambda t: t.matmul(
        ps_b[:, :toks], onesrow_sb[0:1, :], st_mr[0:1, :toks],
        start=True, stop=True), deps=[i7])
    outs = []
    for c in range(NCHUNK):
        o1 = E.emit("vector", (lambda c: lambda v: v.tensor_mul(
            out_chunks[c], s_chunks[c], ps_a[:, :toks]))(c), deps=[b1])
        o2 = E.emit("vector", (lambda c: lambda v: v.tensor_sub(
            out_chunks[c], out_chunks[c], ps_b[:, :toks]))(c), deps=[o1, b2])
        outs.append(o2)
    return outs


def build_ffn(nc, E, es, xin, xout, wg, wu, wd, sg, su, sd, nc_consts,
              NI=64, prefix="f0_", deps0=()):
    TT = 512
    NT = TOK // TT
    sb = {}
    def sbuf(name, shape, dtype):
        sb[name] = es.enter_context(nc.sbuf_tensor(prefix + name, shape, dtype))
        return sb[name]
    xf = sbuf("xf", [128, NCHUNK, TT], F32)
    xb = sbuf("xb", [128, NCHUNK, TT], BF16)
    wgb = sbuf("wgb", [128, 2, NCHUNK, 128], BF16)
    wub = sbuf("wub", [128, 2, NCHUNK, 128], BF16)
    wdb = sbuf("wdb", [128, 2, NI, 128], BF16)
    gate = sbuf("gate", [128, 2, TT], BF16)
    hbuf = sbuf("hbuf", [128, NI, TT], BF16)
    sgc = sbuf("sgc", [128, NI], F32)
    suc = sbuf("suc", [128, NI], F32)
    sdc = sbuf("sdc", [128, NCHUNK], F32)
    sbuf("lnsq", [128, TT], F32)
    sbuf("lnsum", [1, TT], F32)
    sbuf("lnsq1", [1, TT], F32)
    sbuf("lnm", [1, TT], F32)
    sbuf("lnr", [1, TT], F32)
    sbuf("lnmr", [1, TT], F32)

    ps = {}
    def psum(name, shape):
        ps[name] = es.enter_context(nc.psum_tensor(prefix + name, shape, F32))
        return ps[name]
    ps_g = [psum("g0", [128, TT]), psum("g1", [128, TT])]
    ps_u = [psum("u0", [128, TT]), psum("u1", [128, TT])]
    ps_d = [psum("d0", [128, TT]), psum("d1", [128, TT])]
    ps_a = psum("bca", [128, TT])
    ps_b = psum("bcb", [128, TT])
    ps["stat1"] = ps_a[0:1, 0:TT]
    ps["stat2"] = ps_b[0:1, 0:TT]
    ps["bca"] = ps_a
    ps["bcb"] = ps_b

    d_sg = E.dma("sync", sgc[:, :], sg, deps=deps0)
    d_su = E.dma("sync", suc[:, :], su, deps=deps0)
    d_sd = E.dma("sync", sdc[:, :], sd, deps=deps0)

    lv = E.begin_loop(NT)
    first = list(deps0) + [d_sg, d_su, d_sd]

    def dma_x(eng):
        return eng.dma_start(
            out=xf[:, :, :],
            in_=xin[:, bass.ds(lv.val * TT, TT)].rearrange(
                "(c p) n -> p c n", p=128))
    d_x = E.emit("sync", dma_x, deps=first, inc=16)
    c_x = E.emit("vector", lambda v: v.tensor_copy(
        xb[:, :, :].rearrange("p c n -> p (c n)"),
        xf[:, :, :].rearrange("p c n -> p (c n)")), deps=[d_x])

    h_steps = []
    wg_dmas = []
    wu_dmas = []
    sg_steps = []
    mg_last = []
    mu_last = []
    for i in range(NI):
        b = i % 2
        d_wg = E.dma("gpsimd", wgb[:, b, :, :], wg[i],
                     deps=(mg_last[i - 2:i - 1] or first))
        d_wu = E.dma("gpsimd", wub[:, b, :, :], wu[i],
                     deps=(mu_last[i - 2:i - 1] or first))
        wg_dmas.append(d_wg); wu_dmas.append(d_wu)
        mg = []
        for ki in range(NCHUNK):
            mg.append(E.emit("tensor", (lambda ki=ki, b=b: lambda t_: t_.matmul(
                ps_g[b][:, :], wgb[:, b, ki, :], xb[:, ki, :],
                start=(ki == 0), stop=(ki == NCHUNK - 1)))(),
                deps=[d_wg, c_x] + (sg_steps[i - 2:i - 1] or [])))
        mg_last.append(mg[-1])
        sg_i = E.emit("scalar", (lambda i=i, b=b: lambda sc: sc.activation(
            gate[:, b, :], ps_g[b][:, :], AF.Silu, scale=sgc[:, i:i + 1]))(),
            deps=[mg[-1]] + (h_steps[i - 2:i - 1] or []))
        sg_steps.append(sg_i)
        mu = []
        for ki in range(NCHUNK):
            mu.append(E.emit("tensor", (lambda ki=ki, b=b: lambda t_: t_.matmul(
                ps_u[b][:, :], wub[:, b, ki, :], xb[:, ki, :],
                start=(ki == 0), stop=(ki == NCHUNK - 1)))(),
                deps=[d_wu, c_x] + (h_steps[i - 2:i - 1] or [])))
        mu_last.append(mu[-1])
        h_i = E.emit("vector", (lambda i=i, b=b: lambda v: v.scalar_tensor_tensor(
            hbuf[:, i, :], ps_u[b][:, :], suc[:, i:i + 1], gate[:, b, :],
            ALU.mult, ALU.mult))(), deps=[mu[-1], sg_i])
        h_steps.append(h_i)

    s_steps = []
    wd_dmas = []
    md_last = []
    for m in range(NCHUNK):
        b = m % 2
        d_wd = E.dma("gpsimd", wdb[:, b, :, :], wd[m],
                     deps=(md_last[m - 2:m - 1] or h_steps[-1:]))
        wd_dmas.append(d_wd)
        md = []
        for i in range(NI):
            md.append(E.emit("tensor", (lambda i=i, b=b: lambda t_: t_.matmul(
                ps_d[b][:, :], wdb[:, b, i, :], hbuf[:, i, :],
                start=(i == 0), stop=(i == NI - 1)))(),
                deps=[d_wd, h_steps[i]] + (s_steps[m - 2:m - 1] or [])))
        md_last.append(md[-1])
        ev = E.emit("vector", (lambda m=m, b=b: lambda v: v.scalar_tensor_tensor(
            xf[:, m, :], ps_d[b][:, :], sdc[:, m:m + 1], xf[:, m, :],
            ALU.mult, ALU.add))(), deps=[md[-1], d_x])
        s_steps.append(ev)

    s_chunks = [xf[:, c, :] for c in range(NCHUNK)]
    ln_outs = emit_layernorm(E, sb, ps, TT, s_chunks, s_chunks,
                             s_steps, nc_consts)

    def dma_out(eng):
        return eng.dma_start(
            out=xout[:, bass.ds(lv.val * TT, TT)].rearrange(
                "(c p) n -> p c n", p=128),
            in_=xf[:, :, :])
    d_out = E.emit("sync", dma_out, deps=ln_outs, inc=16)
    E.end_loop()
    E.steps[d_x]["deps"] += [(d_out, -1)] + [(x, -1) for x in ln_outs]
    E.steps[wg_dmas[0]]["deps"] += [(mg_last[-2], -1)]
    E.steps[wg_dmas[1]]["deps"] += [(mg_last[-1], -1)]
    E.steps[wu_dmas[0]]["deps"] += [(mu_last[-2], -1)]
    E.steps[wu_dmas[1]]["deps"] += [(mu_last[-1], -1)]
    E.steps[wd_dmas[0]]["deps"] += [(md_last[-2], -1)]
    E.steps[wd_dmas[1]]["deps"] += [(md_last[-1], -1)]
    E.steps[E.steps.index(E.steps[0]) if False else 0]["deps"]
    return [d_out] + ln_outs


def _tiles_lhsT(w_t, km, mm):
    K, M = w_t.shape
    assert K % km == 0 and M % mm == 0
    return np.ascontiguousarray(
        w_t.reshape(K // km, km, M // mm, mm).transpose(2, 1, 0, 3))


def prep_ffn0(inputs, b0, b1):
    wg = _tiles_lhsT(np.asarray(inputs["L0_ffn_w_gate"]), 128, 128)
    wu = _tiles_lhsT(np.asarray(inputs["L0_ffn_w_up"]), 128, 128)
    wd = _tiles_lhsT(np.asarray(inputs["L0_ffn_w_down"]), 128, 128)
    sg = np.ascontiguousarray(np.asarray(inputs["L0_ffn_s_gate"]).reshape(64, 128).T)
    su = np.ascontiguousarray(np.asarray(inputs["L0_ffn_s_up"]).reshape(64, 128).T)
    sd = np.ascontiguousarray(np.asarray(inputs["L0_ffn_s_down"]).reshape(16, 128).T)
    return dict(wg0=wg, wu0=wu, wd0=wd, sg0=sg, su0=su, sd0=sd)


def build_ffn0_program():
    nc = bass.Bass("TRN2", target_bir_lowering=False, debug=False)
    xin = nc.dram_tensor("xin", [D_MODEL, TOK], F32, kind="ExternalInput").ap()
    xout = nc.dram_tensor("xout", [D_MODEL, TOK], F32, kind="ExternalOutput").ap()
    wg = nc.dram_tensor("wg0", [64, 128, 16, 128], I8, kind="ExternalInput").ap()
    wu = nc.dram_tensor("wu0", [64, 128, 16, 128], I8, kind="ExternalInput").ap()
    wd = nc.dram_tensor("wd0", [16, 128, 64, 128], I8, kind="ExternalInput").ap()
    sg = nc.dram_tensor("sg0", [128, 64], F32, kind="ExternalInput").ap()
    su = nc.dram_tensor("su0", [128, 64], F32, kind="ExternalInput").ap()
    sd = nc.dram_tensor("sd0", [128, 16], F32, kind="ExternalInput").ap()
    onescol = nc.inline_tensor(np.ones((128, 1), np.float32), name="onescol_c")
    onesrow = nc.inline_tensor(np.ones((1, 128), np.float32), name="onesrow_c")
    with ExitStack() as es:
        onescol_sb = es.enter_context(nc.sbuf_tensor("onescol", [128, 1], F32))
        onesrow_sb = es.enter_context(nc.sbuf_tensor("onesrow", [1, 128], F32))
        nc_consts = dict(onescol_sb=onescol_sb, onesrow_sb=onesrow_sb)
        E = Emit(serial=False)
        dc1 = E.dma("sync", onescol_sb[:, :], onescol.ap())
        dc2 = E.dma("sync", onesrow_sb[0:1, :], onesrow.ap())
        fin = build_ffn(nc, E, es, xin, xout, wg, wu, wd, sg, su, sd, nc_consts)

        def fini(eng):
            return eng.nop()
        E.emit("sync", fini, deps=fin)
        sems, body = E.lower(nc, es)
        block = es.enter_context(nc.Block())
        block.tensor(lambda t: body("tensor", t))
        block.vector(lambda v: body("vector", v))
        block.scalar(lambda sc: body("scalar", sc))
        block.gpsimd(lambda g: body("gpsimd", g))
        block.sync(lambda s: body("sync", s))
    return nc


def prep_ffn1(inputs):
    gw = np.ascontiguousarray(np.asarray(inputs["L1_ffn_gate_w"]).T)
    uw = np.ascontiguousarray(np.asarray(inputs["L1_ffn_up_w"]).T)
    dw = np.ascontiguousarray(np.asarray(inputs["L1_ffn_down_w"]).T)
    wg = _tiles_lhsT(gw, 128, 128)
    wu = _tiles_lhsT(uw, 128, 128)
    wd = _tiles_lhsT(dw, 128, 128)
    sg = np.ascontiguousarray(np.asarray(inputs["L1_ffn_gate_s"]).reshape(44, 128).T)
    su = np.ascontiguousarray(np.asarray(inputs["L1_ffn_up_s"]).reshape(44, 128).T)
    sd = np.ascontiguousarray(np.asarray(inputs["L1_ffn_down_s"]).reshape(16, 128).T)
    return dict(wg1=wg, wu1=wu, wd1=wd, sg1=sg, su1=su, sd1=sd)


def build_ffn1_program():
    nc = bass.Bass("TRN2", target_bir_lowering=False, debug=False)
    xin = nc.dram_tensor("xin", [D_MODEL, TOK], F32, kind="ExternalInput").ap()
    xout = nc.dram_tensor("xout", [D_MODEL, TOK], F32, kind="ExternalOutput").ap()
    wg = nc.dram_tensor("wg1", [44, 128, 16, 128], F32, kind="ExternalInput").ap()
    wu = nc.dram_tensor("wu1", [44, 128, 16, 128], F32, kind="ExternalInput").ap()
    wd = nc.dram_tensor("wd1", [16, 128, 44, 128], F32, kind="ExternalInput").ap()
    sg = nc.dram_tensor("sg1", [128, 44], F32, kind="ExternalInput").ap()
    su = nc.dram_tensor("su1", [128, 44], F32, kind="ExternalInput").ap()
    sd = nc.dram_tensor("sd1", [128, 16], F32, kind="ExternalInput").ap()
    onescol = nc.inline_tensor(np.ones((128, 1), np.float32), name="onescol_c")
    onesrow = nc.inline_tensor(np.ones((1, 128), np.float32), name="onesrow_c")
    with ExitStack() as es:
        onescol_sb = es.enter_context(nc.sbuf_tensor("onescol", [128, 1], F32))
        onesrow_sb = es.enter_context(nc.sbuf_tensor("onesrow", [1, 128], F32))
        nc_consts = dict(onescol_sb=onescol_sb, onesrow_sb=onesrow_sb)
        E = Emit(serial=False)
        E.dma("sync", onescol_sb[:, :], onescol.ap())
        E.dma("sync", onesrow_sb[0:1, :], onesrow.ap())
        fin = build_ffn(nc, E, es, xin, xout, wg, wu, wd, sg, su, sd, nc_consts,
                        NI=44, prefix="f1_")
        def fini(eng):
            return eng.nop()
        E.emit("sync", fini, deps=fin)
        sems, body = E.lower(nc, es)
        block = es.enter_context(nc.Block())
        block.tensor(lambda t: body("tensor", t))
        block.vector(lambda v: body("vector", v))
        block.scalar(lambda sc: body("scalar", sc))
        block.gpsimd(lambda g: body("gpsimd", g))
        block.sync(lambda s: body("sync", s))
    return nc




RSQ = 1.0 / float(np.sqrt(128.0))


def _rope_tables():
    j = np.arange(0, 16, 2, dtype=np.float64) / 16.0
    inv_freq = 1.0 / np.power(10000.0, j)
    t = np.arange(SEQ, dtype=np.float64) / 40.0
    freqs = np.outer(t, inv_freq)
    cos = np.cos(freqs).T
    sin = np.sin(freqs).T
    cos32 = np.tile(cos, (4, 1)).astype(np.float32)
    sin32 = np.tile(sin, (4, 1)).astype(np.float32)
    return cos32, sin32


def _rot_perm():
    perm = []
    for c in range(4):
        for j in range(8):
            for i in range(4):
                pass
        blk = []
        for i in range(4):
            for j in range(8):
                blk.append((4 * c + i) * 32 + j)
        for i in range(4):
            for j in range(8, 16):
                blk.append((4 * c + i) * 32 + j)
        for i in range(4):
            for j in range(16, 32):
                blk.append((4 * c + i) * 32 + j)
        perm.extend(blk)
    return np.array(perm)


def _perm_consts():
    ps = []
    for i in range(4):
        P = np.zeros((128, 32), np.float32)
        for m in range(8):
            P[i * 8 + m, m] = 1
        for m in range(8, 16):
            P[32 + i * 8 + (m - 8), m] = 1
        for m in range(16, 32):
            P[64 + i * 16 + (m - 16), m] = 1
        ps.append(P.astype(BF))
    return ps


def prep_attn1(inputs):
    g = lambda k: np.asarray(inputs[k])
    perm = _rot_perm()
    out = {}
    out["wdkv"] = _tiles_lhsT(g("L1_attn_W_dkv"), 128, 128)
    out["wdq"] = _tiles_lhsT(g("L1_attn_W_dq"), 128, 128)
    out["bdkv"] = np.ascontiguousarray(g("L1_attn_b_dkv").reshape(4, 128).T)
    out["bdq"] = np.ascontiguousarray(g("L1_attn_b_dq").reshape(4, 128).T)
    wkr = g("L1_attn_W_kr")[:, perm]
    out["wkr"] = _tiles_lhsT(wkr, 128, 128)
    out["bkr"] = np.ascontiguousarray(g("L1_attn_b_kr")[perm].reshape(4, 128).T)
    wqr = g("L1_attn_W_qr")[:, perm]
    out["wqr"] = _tiles_lhsT(wqr, 128, 128)
    out["bqr"] = np.ascontiguousarray(g("L1_attn_b_qr")[perm].reshape(4, 128).T)
    wuk = g("L1_attn_W_uk")
    wk = np.zeros((16, 128, 4, 128), np.float32)
    wk[:, :, :, 0:96] = wuk.reshape(4, 128, 16, 96).transpose(2, 1, 0, 3)
    out["wuk"] = wk
    bk = np.zeros((128, 16), np.float32)
    bk[0:96, :] = g("L1_attn_b_uk").reshape(16, 96).T
    out["buk"] = bk
    wuq = g("L1_attn_W_uq")
    wq = np.zeros((16, 128, 4, 128), np.float32)
    wq[:, :, :, 0:96] = wuq.reshape(4, 128, 16, 96).transpose(2, 1, 0, 3)
    out["wuq"] = wq
    bq = np.zeros((128, 16), np.float32)
    bq[0:96, :] = g("L1_attn_b_uq").reshape(16, 96).T
    out["buq"] = bq
    wuv = g("L1_attn_W_uv")
    out["wuv"] = np.ascontiguousarray(
        wuv.reshape(4, 128, 16, 128).transpose(2, 1, 0, 3))
    out["buv"] = np.ascontiguousarray(g("L1_attn_b_uv").reshape(16, 128).T)
    out["wout1"] = _tiles_lhsT(g("L1_attn_W_out"), 128, 128)
    out["bout1"] = np.ascontiguousarray(g("L1_attn_b_out").reshape(16, 128).T)
    return out


def attn1_dram(nc):
    d = {}
    def inp(name, shape, dtype=F32):
        d[name] = nc.dram_tensor(name, list(shape), dtype, kind="ExternalInput").ap()
    inp("wdkv", [4, 128, 16, 128]); inp("wdq", [4, 128, 16, 128])
    inp("bdkv", [128, 4]); inp("bdq", [128, 4])
    inp("wkr", [4, 128, 16, 128]); inp("bkr", [128, 4])
    inp("wqr", [4, 128, 4, 128]); inp("bqr", [128, 4])
    inp("wuk", [16, 128, 4, 128]); inp("buk", [128, 16])
    inp("wuq", [16, 128, 4, 128]); inp("buq", [128, 16])
    inp("wuv", [16, 128, 4, 128]); inp("buv", [128, 16])
    inp("wout1", [16, 128, 16, 128]); inp("bout1", [128, 16])
    return d


def build_attn1(nc, E, es, xin, xout, dr, nc_consts, prefix="a1_", deps0=()):
    cos32, sin32 = _rope_tables()
    c_cos = nc.inline_tensor(cos32, name="c_cos")
    c_sin = nc.inline_tensor(sin32, name="c_sin")
    perms = _perm_consts()
    permpads = []
    for p in perms:
        pp = np.zeros((128, 128), np.float32)
        pp[:, 96:128] = p.astype(np.float32)
        permpads.append(pp.astype(BF))
    c_perm = [nc.inline_tensor(p, name=f"c_perm{i}") for i, p in enumerate(permpads)]
    tribig = np.triu(np.full((128, 128), -1e30, np.float32), 1)
    c_tribig = nc.inline_tensor(tribig, name="c_tribig")
    c_identb = nc.inline_tensor(np.eye(128, dtype=BF), name="c_identb")
    c_onesb = nc.inline_tensor(np.ones((1, 128), BF), name="c_onesb")
    ofm_d = nc.dram_tensor(prefix + "ofm_d", [2048, SEQ], BF16, kind="Internal").ap()

    sb = {}
    def sbuf(name, shape, dtype):
        sb[name] = es.enter_context(nc.sbuf_tensor(prefix + name, shape, dtype))
        return sb[name]
    cosb = sbuf("cosb", [32, SEQ], F32)
    sinb = sbuf("sinb", [32, SEQ], F32)
    permb = sbuf("permb", [128, 4, 128], BF16)
    tribigb = sbuf("tribigb", [128, 128], F32)
    identb = sbuf("identb", [128, 128], BF16)
    onesrow_b = sbuf("onesrow_b", [1, 128], BF16)
    w_proj = sbuf("w_proj", [128, NCHUNK, 128], BF16)
    w_ukh = sbuf("w_ukh", [128, 4, 128], BF16)
    w_uqh = sbuf("w_uqh", [128, 4, 128], BF16)
    w_uvh = sbuf("w_uvh", [128, 4, 128], BF16)
    w_outm = sbuf("w_outm", [128, NCHUNK, 128], BF16)
    b_dkv = sbuf("b_dkv", [128, 4], F32)
    b_dq = sbuf("b_dq", [128, 4], F32)
    b_kr = sbuf("b_kr", [128, 4], F32)
    b_qr = sbuf("b_qr", [128, 4], F32)
    b_uk = sbuf("b_uk", [128, 16], F32)
    b_uq = sbuf("b_uq", [128, 16], F32)
    b_uvb = sbuf("b_uvb", [128, 16], F32)
    v_fm = sbuf("v_fm", [128, SEQ], BF16)
    b_out = sbuf("b_out", [128, 16], F32)
    xb = sbuf("xb", [128, NCHUNK, 512], BF16)
    ckv = sbuf("ckv", [128, 4, SEQ], BF16)
    cq = sbuf("cq", [128, 4, SEQ], BF16)
    kro = sbuf("kro", [128, 4, SEQ], BF16)
    qro = sbuf("qro", [128, 4, SEQ], BF16)
    obuf = sbuf("obuf", [128, SEQ], BF16)
    ofmt = sbuf("ofmt", [128, NCHUNK, 512], BF16)
    rtmp = sbuf("rtmp", [128, 512], F32)
    rt1 = sbuf("rt1", [32, 512], F32)
    rt2 = sbuf("rt2", [32, 512], F32)
    rt3 = sbuf("rt3", [32, 512], F32)
    qb_h = sbuf("qb_h", [128, SEQ], BF16)
    kb_h = sbuf("kb_h", [128, SEQ], BF16)
    v_h = sbuf("v_h", [128, 16, 128], BF16)
    p_sb = sbuf("p_sb", [128, SEQ], BF16)
    pt_sb = sbuf("pt_sb", [128, 2, 512], BF16)
    negm = sbuf("negm", [128, 1], F32)
    lsum = sbuf("lsum", [128, 1], F32)
    rl = sbuf("rl", [128, 1], F32)
    xf = sbuf("xf", [128, NCHUNK, 512], F32)
    sbuf("lnr", [1, 512], F32)
    sb["lnsq"] = rtmp
    sb["lnsum"] = rt1
    sb["lnsq1"] = rt2
    sb["lnm"] = rt3

    ps = {}
    def psum(name, shape, dtype=F32):
        ps[name] = es.enter_context(nc.psum_tensor(prefix + name, shape, dtype))
        return ps[name]
    big = psum("big", [128, 2048])
    ps_pt = psum("pt", [128, 512], BF16)
    ps_o = psum("o", [128, 128])
    ps_p = psum("p", [128, 512])
    ps["d"] = big[:, 0:512]
    ps["stat1"] = big[0:1, 512:1024]
    ps["stat2"] = big[0:1, 1024:1536]
    ps["bca"] = big[:, 1024:1536]
    ps["bcb"] = big[:, 1536:2048]

    ld = list(deps0)
    ld.append(E.dma("sync", cosb[:, :], c_cos.ap(), deps=deps0))
    ld.append(E.dma("sync", sinb[:, :], c_sin.ap(), deps=deps0))
    for i in range(4):
        ld.append(E.dma("sync", permb[:, i, :], c_perm[i].ap()))
    ld.append(E.dma("sync", tribigb[:, :], c_tribig.ap()))
    ld.append(E.dma("sync", identb[:, :], c_identb.ap()))
    ld.append(E.dma("sync", onesrow_b[0:1, :], c_onesb.ap()))
    ld.append(E.dma("sync", b_dkv[:, :], dr["bdkv"]))
    ld.append(E.dma("sync", b_dq[:, :], dr["bdq"]))
    ld.append(E.dma("sync", b_kr[:, :], dr["bkr"]))
    ld.append(E.dma("sync", b_qr[:, :], dr["bqr"]))
    ld.append(E.dma("sync", b_uk[:, :], dr["buk"]))
    ld.append(E.dma("sync", b_uq[:, :], dr["buq"]))
    ld.append(E.dma("sync", b_uvb[:, :], dr["buv"], deps=deps0))
    ld.append(E.dma("sync", b_out[:, :], dr["bout1"]))

    fin_prev = list(deps0)
    for s in range(2):
        base = s * SEQ
        ckv_done = []
        rope_done = []
        prev_tile = list(fin_prev)
        prev_wp = list(ld)
        for tt in range(4):
            tsl = slice(base + tt * 512, base + (tt + 1) * 512)
            lsl = slice(tt * 512, (tt + 1) * 512)
            d_x = E.dma("gpsimd", xb[:, :, :],
                        xin[:, tsl].rearrange("(c p) n -> p c n", p=128),
                        deps=prev_tile + ld)
            tile_deps = []
            for which, wnm, bt, dst in (("kv", "wdkv", b_dkv, ckv),
                                        ("q", "wdq", b_dq, cq),
                                        ("kr", "wkr", b_kr, kro)):
                for m in range(4):
                    d_w = E.dma("gpsimd", w_proj[:, :, :],
                                dr[wnm][m],
                                deps=prev_wp)
                    mm = []
                    for ki in range(NCHUNK):
                        mm.append(E.emit("tensor", (lambda m=m, ki=ki: lambda t_: t_.matmul(
                            ps_p[:, :], w_proj[:, ki, :], xb[:, ki, :],
                            start=(ki == 0), stop=(ki == NCHUNK - 1)))(),
                            deps=[d_x, d_w] + tile_deps[-1:]))
                    prev_wp = [mm[-1]]
                    if which != "kr":
                        ev = E.emit("vector", (lambda bt=bt, dst=dst, m=m, lsl=lsl: lambda v: v.tensor_scalar(
                            dst[:, m, lsl], ps_p[:, :], bt[:, m:m + 1], None, ALU.add))(),
                            deps=[mm[-1]])
                        tile_deps = [ev]
                        ckv_done.append(ev)
                    else:
                        ev = E.emit("vector", (lambda bt=bt, m=m: lambda v: v.tensor_scalar(
                            rtmp[:, :], ps_p[:, :], bt[:, m:m + 1], None, ALU.add))(),
                            deps=[mm[-1]])
                        tile_deps = _emit_rope(E, rtmp, rt1, rt2, rt3, cosb, sinb,
                                               kro, m, lsl, ev, rope_done)
            for m in range(4):
                d_w = E.dma("gpsimd", w_proj[:, 0:4, :],
                            dr["wqr"][m],
                            deps=prev_wp)
                mm = []
                for ki in range(4):
                    mm.append(E.emit("tensor", (lambda m=m, ki=ki, lsl=lsl: lambda t_: t_.matmul(
                        ps_p[:, :], w_proj[:, ki, :], cq[:, ki, lsl],
                        start=(ki == 0), stop=(ki == 3)))(),
                        deps=[d_w] + tile_deps[-1:] + ckv_done[-1:]))
                prev_wp = [mm[-1]]
                ev = E.emit("vector", (lambda m=m: lambda v: v.tensor_scalar(
                    rtmp[:, :], ps_p[:, :], b_qr[:, m:m + 1], None,
                    ALU.add))(), deps=[mm[-1]])
                tile_deps = _emit_rope(E, rtmp, rt1, rt2, rt3, cosb, sinb,
                                       qro, m, lsl, ev, rope_done)
            prev_tile = tile_deps
        proj_done = ckv_done[-1:] + rope_done[-2:]

        head_done = list(proj_done)
        o_dmas = []
        for h in range(16):
            hp = []
            d_uq = E.dma("gpsimd", w_uqh[:, :, :],
                         dr["wuq"][h],
                         deps=head_done[-1:] + proj_done)
            d_uk = E.dma("gpsimd", w_ukh[:, :, :],
                         dr["wuk"][h],
                         deps=head_done[-1:] + proj_done)
            d_uv = E.dma("gpsimd", w_uvh[:, :, :],
                         dr["wuv"][h],
                         deps=head_done[-1:] + proj_done)
            for wt, bt, src, ro, dst, d_w, qsc in (
                    (w_uqh, b_uq, cq, qro, qb_h, d_uq, True),
                    (w_ukh, b_uk, ckv, kro, kb_h, d_uk, False)):
                for tt2 in range(4):
                    l2 = slice(tt2 * 512, (tt2 + 1) * 512)
                    mm = []
                    for ki in range(4):
                        mm.append(E.emit("tensor", (lambda wt=wt, ki=ki, src=src, l2=l2: lambda t_: t_.matmul(
                            ps_p[:, :], wt[:, ki, :], src[:, ki, l2],
                            start=(ki == 0), stop=False))(),
                            deps=[d_w] + head_done[-1:] + hp[-1:]))
                    mm.append(E.emit("tensor", (lambda ro=ro, h=h, l2=l2: lambda t_: t_.matmul(
                        ps_p[:, :], permb[:, h % 4, :], ro[:, h // 4, l2],
                        start=False, stop=True))(),
                        deps=rope_done[-1:]))
                    if qsc:
                        ev = E.emit("vector", (lambda bt=bt, h=h, dst=dst, l2=l2: lambda v: v.tensor_scalar(
                            dst[:, l2], ps_p[:, :], bt[:, h:h + 1], RSQ,
                            ALU.add, ALU.mult))(), deps=[mm[-1]])
                    else:
                        ev = E.emit("vector", (lambda bt=bt, h=h, dst=dst, l2=l2: lambda v: v.tensor_scalar(
                            dst[:, l2], ps_p[:, :], bt[:, h:h + 1], None,
                            ALU.add))(), deps=[mm[-1]])
                    hp = [ev]
            for tt2 in range(4):
                l2 = slice(tt2 * 512, (tt2 + 1) * 512)
                mm = []
                for ki in range(4):
                    mm.append(E.emit("tensor", (lambda ki=ki, l2=l2: lambda t_: t_.matmul(
                        ps_p[:, :], w_uvh[:, ki, :], ckv[:, ki, l2],
                        start=(ki == 0), stop=(ki == 3)))(),
                        deps=[d_uv] + hp[-1:]))
                ev = E.emit("vector", (lambda h=h, l2=l2: lambda v: v.tensor_scalar(
                    v_fm[:, l2], ps_p[:, :], b_uvb[:, h:h + 1], None, ALU.add))(),
                    deps=[mm[-1]])
                hp = [ev]
            for q4 in range(4):
                trs = []
                for j in range(4):
                    kb = 4 * q4 + j
                    trs.append(E.emit("tensor", (lambda kb=kb, j=j: lambda t_: t_.matmul(
                        ps_pt[:, j * 128:(j + 1) * 128],
                        v_fm[:, kb * 128:(kb + 1) * 128], identb[:, :],
                        is_transpose=True, start=True, stop=True))(),
                        deps=hp[-1:]))
                ev = E.emit("vector", (lambda q4=q4: lambda v: v.tensor_copy(
                    v_h[:, 4 * q4:4 * q4 + 4, :],
                    ps_pt[:, 0:512].rearrange("p (a b) -> p a b", a=4)))(),
                    deps=[trs[-1]])
                hp = [ev]
            qkv_done = list(hp)

            prev_ex = []
            prev_oev = []
            for qi in range(16):
                nkg = qi // 4 + 1
                klen = (qi + 1) * 128
                qsl = slice(qi * 128, (qi + 1) * 128)
                smm = []
                for kg in range(nkg):
                    kw = min(512, klen - kg * 512)
                    ksl = slice(kg * 512, kg * 512 + kw)
                    m1 = E.emit("tensor", (lambda qsl=qsl, ksl=ksl: lambda t_: t_.matmul(
                        big[:, ksl], qb_h[:, qsl], kb_h[:, ksl],
                        start=True, stop=True))(), deps=qkv_done + smm[-1:] + prev_ex)
                    smm.append(m1)
                dmask = E.emit("vector", (lambda qsl2=slice(qi * 128, klen): lambda v: v.tensor_add(
                    big[:, qsl2], big[:, qsl2], tribigb[:, :]))(), deps=[smm[-1]])
                mx = E.emit("vector", (lambda klen=klen: lambda v: v.reduce_max(
                    negm[:, 0:1], big[:, 0:klen], axis=mybir.AxisListType.X,
                    negate=True))(), deps=[dmask] + smm)
                ex = E.emit("scalar", (lambda klen=klen: lambda sc: sc.activation(
                    p_sb[:, 0:klen], big[:, 0:klen], AF.Exp, bias=negm[:, 0:1],
                    accum_out=lsum[:, 0:1]))(), deps=[mx])
                prev_ex = [ex]
                rc = E.emit("vector", lambda v: v.reciprocal(rl[:, 0:1], lsum[:, 0:1]),
                            deps=[ex])
                pscl = E.emit("gpsimd", (lambda klen=klen: lambda g: g.tensor_scalar(
                    p_sb[:, 0:klen], p_sb[:, 0:klen], rl[:, 0:1], None, ALU.mult))(),
                    deps=[rc])
                omm = []
                prev_tev = []
                nq4 = (qi + 1 + 3) // 4
                for g4 in range(nq4):
                    kb0 = 4 * g4
                    nkb = min(4, qi + 1 - kb0)
                    pb = g4 % 2
                    trs = []
                    for j in range(nkb):
                        kb = kb0 + j
                        trs.append(E.emit("tensor", (lambda kb=kb, j=j: lambda t_: t_.matmul(
                            ps_pt[:, j * 128:(j + 1) * 128],
                            p_sb[:, kb * 128:(kb + 1) * 128], identb[:, :],
                            is_transpose=True, start=True, stop=True))(),
                            deps=[pscl] + prev_tev))
                    tev = E.emit("vector", (lambda pb=pb, nkb=nkb: lambda v: v.tensor_copy(
                        pt_sb[:, pb, 0:nkb * 128], ps_pt[:, 0:nkb * 128]))(),
                        deps=[trs[-1]])
                    prev_tev = [tev]
                    for j in range(nkb):
                        kb = kb0 + j
                        omm.append(E.emit("tensor", (lambda kb=kb, j=j, pb=pb, qi=qi: lambda t_: t_.matmul(
                            ps_o[:, :], v_h[:, kb, :],
                            pt_sb[:, pb, j * 128:(j + 1) * 128],
                            start=(kb == 0), stop=(kb == qi)))(),
                            deps=[tev] + omm[-1:]))
                oev = E.emit("vector", (lambda qsl=qsl: lambda v: v.tensor_copy(
                    obuf[:, qsl], ps_o[:, :]))(), deps=[omm[-1]] + prev_oev)
                prev_oev = [oev]
                head_done = [oev]
            d_o = E.dma("sync", ofm_d[h * 128:(h + 1) * 128, :], obuf[:, :],
                        deps=[head_done[-1]])
            o_dmas.append(d_o)
            head_done = [d_o]
        attn_done = list(o_dmas)

        prev = list(attn_done)
        prev_wo = list(attn_done)
        for tt in range(4):
            tsl = slice(base + tt * 512, base + (tt + 1) * 512)
            lsl = slice(tt * 512, (tt + 1) * 512)
            d_xf = E.dma("sync", xf[:, :, :],
                         xin[:, tsl].rearrange("(c p) n -> p c n", p=128),
                         deps=prev)
            d_of = E.dma("sync", ofmt[:, :, :],
                         ofm_d[:, lsl].rearrange("(c p) n -> p c n", p=128),
                         deps=prev + attn_done)
            s_steps = []
            for m in range(NCHUNK):
                d_wo = E.dma("gpsimd", w_outm[:, :, :],
                             dr["wout1"][m],
                             deps=prev_wo)
                mm = []
                for ki in range(NCHUNK):
                    mm.append(E.emit("tensor", (lambda m=m, ki=ki: lambda t_: t_.matmul(
                        ps["d"], w_outm[:, ki, :], ofmt[:, ki, :],
                        start=(ki == 0), stop=(ki == NCHUNK - 1)))(),
                        deps=[d_xf, d_of, d_wo] + s_steps[-1:]))
                prev_wo = [mm[-1]]
                ev = E.emit("vector", (lambda m=m: lambda v: v.scalar_tensor_tensor(
                    xf[:, m, :], ps["d"], b_out[:, m:m + 1], xf[:, m, :],
                    ALU.add, ALU.add))(), deps=[mm[-1], d_xf])
                s_steps.append(ev)
            s_chunks = [xf[:, c, :] for c in range(NCHUNK)]
            ln_outs = emit_layernorm(E, sb, ps, 512, s_chunks, s_chunks,
                                     s_steps, nc_consts)
            d_out = E.dma("sync", xout[:, tsl].rearrange("(c p) n -> p c n", p=128),
                          xf[:, :, :], deps=ln_outs)
            prev = [d_out] + ln_outs
        fin_prev = prev
    return fin_prev


def _emit_rope(E, rtmp, rt1, rt2, rt3, cosb, sinb, dst, m, lsl, ev, rope_done):
    r0 = E.emit("vector", (lambda: lambda v: v.tensor_copy(
        rt3[:, :], rtmp[32:64, :]))(), deps=[ev])
    r1 = E.emit("vector", (lambda: lambda v: v.tensor_mul(
        rt1[:, :], rtmp[0:32, :], cosb[:, lsl]))(), deps=[ev])
    r2 = E.emit("vector", (lambda: lambda v: v.tensor_mul(
        rt2[:, :], rt3[:, :], sinb[:, lsl]))(), deps=[r0])
    r3 = E.emit("vector", (lambda: lambda v: v.tensor_sub(
        dst[0:32, m, lsl], rt1[:, :], rt2[:, :]))(), deps=[r1, r2])
    r4 = E.emit("vector", (lambda: lambda v: v.tensor_mul(
        rt1[:, :], rt3[:, :], cosb[:, lsl]))(), deps=[r3])
    r5 = E.emit("vector", (lambda: lambda v: v.tensor_mul(
        rt2[:, :], rtmp[0:32, :], sinb[:, lsl]))(), deps=[r3])
    r6 = E.emit("vector", (lambda: lambda v: v.tensor_add(
        dst[32:64, m, lsl], rt1[:, :], rt2[:, :]))(), deps=[r4, r5])
    r7 = E.emit("vector", (lambda: lambda v: v.tensor_copy(
        dst[64:128, m, lsl], rtmp[64:128, :]))(), deps=[ev, r4, r5])
    rope_done += [r6, r7]
    return [r6, r7]


def build_attn1_program():
    nc = bass.Bass("TRN2", target_bir_lowering=False, debug=False)
    xin = nc.dram_tensor("xin", [D_MODEL, TOK], F32, kind="ExternalInput").ap()
    xout = nc.dram_tensor("xout", [D_MODEL, TOK], F32, kind="ExternalOutput").ap()
    dr = attn1_dram(nc)
    onescol = nc.inline_tensor(np.ones((128, 1), np.float32), name="onescol_c")
    onesrow = nc.inline_tensor(np.ones((1, 128), np.float32), name="onesrow_c")
    with ExitStack() as es:
        onescol_sb = es.enter_context(nc.sbuf_tensor("onescol", [128, 1], F32))
        onesrow_sb = es.enter_context(nc.sbuf_tensor("onesrow", [1, 128], F32))
        nc_consts = dict(onescol_sb=onescol_sb, onesrow_sb=onesrow_sb)
        E = Emit(serial=False)
        E.dma("sync", onescol_sb[:, :], onescol.ap())
        E.dma("sync", onesrow_sb[0:1, :], onesrow.ap())
        fin = build_attn1(nc, E, es, xin, xout, dr, nc_consts)
        def fini(eng):
            return eng.nop()
        E.emit("sync", fini, deps=fin)
        sems, body = E.lower(nc, es)
        block = es.enter_context(nc.Block())
        block.tensor(lambda t: body("tensor", t))
        block.vector(lambda v: body("vector", v))
        block.scalar(lambda sc: body("scalar", sc))
        block.gpsimd(lambda g: body("gpsimd", g))
        block.sync(lambda s: body("sync", s))
    return nc




RMS_EPS = 1.1920929e-07
NH = 64
HD = 64
DST = 128
DIN = 4096
NXC = 32
NBC = 34
NPC = 35


def prep_mamba(inputs):
    g = lambda k: np.asarray(inputs[k])
    W_in = g("L0_attn_W_in")
    WT = np.ascontiguousarray(W_in.T)
    w_z = _tiles_lhsT(WT[:, :DIN], 128, 128)
    xbc = WT[:, DIN:]
    xbc_pad = np.zeros((2048, NPC * 128), np.float32)
    xbc_pad[:, :4416] = xbc
    w_xbc = _tiles_lhsT(xbc_pad, 128, 128)
    W_out = g("L0_attn_W_out")
    w_o = _tiles_lhsT(np.ascontiguousarray(W_out.T), 128, 128)
    cw = g("L0_attn_conv_w")
    cw_pad = np.zeros((NPC * 128, 4), np.float32)
    cw_pad[:4352] = cw
    convw = np.ascontiguousarray(cw_pad.reshape(NPC, 128, 4).transpose(1, 0, 2))
    cb = g("L0_attn_conv_b")
    cb_pad = np.zeros((NPC * 128,), np.float32)
    cb_pad[:4352] = cb
    convb = np.ascontiguousarray(cb_pad.reshape(NPC, 128).T)
    dtb = g("L0_attn_dt_bias").reshape(64, 1)
    alog = g("L0_attn_A_log").reshape(64, 1)
    D = g("L0_attn_D")
    drep = np.zeros((128, 32), np.float32)
    for c in range(32):
        drep[0:64, c] = D[2 * c]
        drep[64:128, c] = D[2 * c + 1]
    nw = np.ascontiguousarray(g("L0_attn_norm_w").reshape(32, 128).T)
    return dict(w_z=w_z, w_xbc=w_xbc, w_o=w_o, convw=convw, convb=convb,
                dtb=dtb, alog=alog, drep=drep, nw=nw)


def mamba_dram(nc):
    d = {}
    def inp(name, shape, dtype=F32):
        d[name] = nc.dram_tensor(name, list(shape), dtype, kind="ExternalInput").ap()
    inp("w_z", [32, 128, 16, 128]); inp("w_xbc", [NPC, 128, 16, 128])
    inp("w_o", [16, 128, 32, 128])
    inp("convw", [128, NPC, 4]); inp("convb", [128, NPC])
    inp("dtb", [64, 1]); inp("alog", [64, 1]); inp("drep", [128, 32])
    inp("nw", [128, 32])
    return d


def build_mamba(nc, E, es, xin, xout, dr, nc_consts, prefix="m0_", deps0=()):
    GT = M_GT
    NG = SEQ // GT
    CPG = GT // 128
    trif = np.triu(np.ones((128, 128), np.float32))
    c_trif = nc.inline_tensor(trif, name="m_trif")
    c_trinegf = nc.inline_tensor(-trif, name="m_trinegf")
    c_identf = nc.inline_tensor(np.eye(128, dtype=np.float32), name="m_identf")
    e127 = np.zeros((128, 128), np.float32); e127[127, :] = 1.0
    c_e127f = nc.inline_tensor(e127, name="m_e127f")
    c_identnegf = nc.inline_tensor(-np.eye(128, dtype=np.float32), name="m_identnegf")
    c_identb2 = nc.inline_tensor(np.eye(128, dtype=BF), name="m_identb2")

    sb = {}
    def sbuf(name, shape, dtype):
        sb[name] = es.enter_context(nc.sbuf_tensor(prefix + name, shape, dtype))
        return sb[name]
    trif_b = sbuf("trif", [128, 128], F32)
    trinegf_b = sbuf("trinegf", [128, 128], F32)
    identf_b = sbuf("identf", [128, 128], F32)
    e127_b = sbuf("e127", [128, 128], F32)
    identnegf_b = sbuf("identnegf", [128, 128], F32)
    identb_b = sbuf("identb", [128, 128], BF16)
    convw_b = sbuf("convw", [128, NPC, 4], F32)
    convb_b = sbuf("convb", [128, NPC], F32)
    dtb_b = sbuf("dtb", [64, 1], F32)
    alog_b = sbuf("alog", [64, 1], F32)
    acol = sbuf("acol", [64, 1], F32)
    drep_b = sbuf("drep", [128, 32], F32)
    nw_b = sbuf("nw", [128, 32], F32)
    wbuf = sbuf("wbuf", [128, 2, 32, 128], BF16)
    uf = sbuf("uf", [128, NCHUNK, GT], F32)
    ub = uf[:, :, 0:GT // 2].bitcast(BF16)
    xbc = sbuf("xbc", [128, NBC, GT + 3], BF16)
    xcs = sbuf("xcs", [128, NBC, GT], BF16)

    dt_g = sbuf("dt_g", [64, GT], F32)
    g_g = sbuf("g_g", [64, GT], F32)
    lndt_g = sbuf("lndt_g", [64, GT], F32)
    ta = sbuf("ta", [64, GT], F32)
    hst = sbuf("hst", [128, DIN], F32)
    hbf = sbuf("hbf", [128, DIN], BF16)
    htmp = sbuf("htmp", [128, 128], F32)
    y_g = sbuf("y_g", [128, NXC, GT], BF16)
    x_tm = sbuf("x_tm", [128, NXC, 128], BF16)
    b_tm = sbuf("b_tm", [128, 128], BF16)
    gT = sbuf("gT", [128, 64], F32)
    lndtT = sbuf("lndtT", [128, 64], F32)
    acsT = sbuf("acsT", [128, 64], F32)
    eacsT = sbuf("eacsT", [128, 64], F32)
    decT = sbuf("decT", [128, 64], F32)
    cb_sb = sbuf("cb_sb", [128, 128], BF16)
    em_sb = sbuf("em_sb", [128, 512], F32)
    p_sb4 = sbuf("p_sb4", [128, 512], BF16)
    xt_sb = sbuf("xt_sb", [128, 128], BF16)
    yoff_sb = sbuf("yoff_sb", [128, 128], BF16)
    yofft_sb = sbuf("yofft_sb", [128, 128], BF16)
    gam_sb = sbuf("gam_sb", [128, 256], F32)
    zrow_b = sbuf("zrow_b", [1, 128], BF16)
    zrow_f = sbuf("zrow_f", [1, 128], F32)
    gtmp = sbuf("gtmp", [128, 2, GT], BF16)
    ztmp = sbuf("ztmp", [128, 2, GT], BF16)
    cacc1 = gtmp[:, 0, :]
    cacc2 = ztmp[:, 0, :]
    rms_r = lndt_g
    sb["lnsq"] = em_sb
    sb["lnsum"] = ta
    sb["lnsq1"] = lndt_g
    sb["lnm"] = dt_g
    sb["lnr"] = g_g

    ps = {}
    def psum(name, shape, dtype=F32):
        ps[name] = es.enter_context(nc.psum_tensor(prefix + name, shape, dtype))
        return ps[name]
    ps_proj = psum("proj", [128, GT])
    ps_dcb = psum("dcb", [128, 512])
    ps_s = psum("s", [128, 128])
    ps_y = psum("y", [128, 128])
    ps_u2 = psum("u2", [128, 128])
    ps_trb = psum("trb", [128, 512], BF16)
    ps_f = psum("f", [128, 512])
    ps_ln = psum("ln", [128, 512])
    ps["stat1"] = ps_ln[0:1, 0:512]
    ps["stat2"] = ps_f[0:1, 0:512]
    ps["bca"] = ps_ln[:, 0:512]
    ps["bcb"] = ps_dcb[:, 0:512]
    v_trf = ps_f[:, 0:128]
    v_acs = ps_f[:, 128:192]
    v_dec = ps_f[:, 192:256]
    v_gam = ps_f[:, 256:384]
    v_rms = ps_ln[:, 0:GT]

    ld = list(deps0)
    for t_, c_ in ((trif_b, c_trif), (trinegf_b, c_trinegf), (identf_b, c_identf),
                   (e127_b, c_e127f), (identnegf_b, c_identnegf),
                   (identb_b, c_identb2)):
        ld.append(E.dma("sync", t_[:, :], c_.ap(), deps=deps0))
    for nm, t_ in (("convw", convw_b), ("convb", convb_b), ("dtb", dtb_b),
                   ("alog", alog_b), ("drep", drep_b), ("nw", nw_b)):
        src_ap = dr[nm]
        if nm == "convw":
            src_ap = src_ap.rearrange("p c j -> p (c j)")
            ld.append(E.dma("sync", t_[:, :, :].rearrange("p c j -> p (c j)"), src_ap, deps=deps0))
        else:
            ld.append(E.dma("sync", t_[:, :] if t_.shape[1] > 1 else t_[:, 0:1], src_ap, deps=deps0))
    a1 = E.emit("scalar", lambda sc: sc.activation(acol[:, 0:1], alog_b[:, 0:1], AF.Exp),
                deps=ld)
    az = E.emit("vector", lambda v: v.memset(zrow_b[0:1, :], 0.0), deps=[])
    az2 = E.emit("vector", lambda v: v.memset(zrow_f[0:1, :], 0.0), deps=[])
    ld.append(az)
    ld.append(az2)
    a2 = E.emit("vector", lambda v: v.tensor_scalar(
        acol[:, 0:1], acol[:, 0:1], -1.0, None, ALU.mult), deps=[a1])
    ld.append(a2)

    fin_prev = list(deps0)
    for s in range(2):
        base = s * SEQ
        mz1 = E.emit("vector", lambda v: v.memset(hst[:, :], 0.0), deps=fin_prev + ld)
        mz2 = E.emit("vector", lambda v: v.memset(
            xbc[:, :, 0:3], 0.0), deps=fin_prev + ld)
        seq_prev = [mz1, mz2]
        if s == 0:
            prev_w = []
        lv = E.begin_loop(NG)
        for gi in range(1):
            def dma_u(eng, base=base, lv=lv):
                return eng.dma_start(
                    out=ub[:, :, :],
                    in_=xin[:, bass.ds(base + lv.val * GT, GT)].rearrange(
                        "(c p) n -> p c n", p=128))
            d_u = E.emit("gpsimd", dma_u, deps=seq_prev, inc=16)
            prev_ev = list(seq_prev)
            dt_done = []
            evp = {0: list(seq_prev), 1: list(seq_prev)}
            wlast = {0: prev_w, 1: prev_w}
            in_dw = []
            in_mm = []
            for m in range(NPC):
                b = m % 2
                pp = ps_proj if b == 0 else ps_dcb
                d_w = E.dma("gpsimd", wbuf[:, b, 0:16, :],
                            dr["w_xbc"][m],
                            deps=wlast[b])
                in_dw.append(d_w)
                mm = []
                for ki in range(NCHUNK):
                    mm.append(E.emit("tensor", (lambda m=m, ki=ki, b=b, pp=pp: lambda t_: t_.matmul(
                        pp[:, 0:GT], wbuf[:, b, ki, :], ub[:, ki, :],
                        start=(ki == 0), stop=(ki == NCHUNK - 1)))(),
                        deps=[d_u, d_w] + evp[b][-1:]))
                wlast[b] = [mm[-1]]
                in_mm.append(mm[-1])
                prev_w = [mm[-1]]
                if m < NBC:
                    ev = E.emit("vector", (lambda m=m, pp=pp: lambda v: v.tensor_copy(
                        xbc[:, m, 3:3 + GT], pp[:, 0:GT]))(), deps=[mm[-1]])
                    evp[b] = [ev]
                    prev_ev = [ev]
                else:
                    e1 = E.emit("vector", lambda v: v.tensor_scalar(
                        dt_g[:, :], ps_proj[0:64, :], dtb_b[:, 0:1], 0.0,
                        ALU.add, ALU.max), deps=[mm[-1]])
                    if not (M_STAGES & 8):
                        prev_ev = [e1]
                        dt_done = [e1]
                        continue
                    e2 = E.emit("scalar", lambda sc: sc.activation(
                        ta[:, :], ps_proj[0:64, :], AF.Abs, bias=dtb_b[:, 0:1]),
                        deps=[mm[-1], e1])
                    e3 = E.emit("scalar", lambda sc: sc.activation(
                        ta[:, :], ta[:, :], AF.Exp, scale=-1.0), deps=[e2])
                    e4 = E.emit("scalar", lambda sc: sc.activation(
                        ta[:, :], ta[:, :], AF.Ln, bias=1.0), deps=[e3])
                    e5 = E.emit("vector", lambda v: v.tensor_add(
                        dt_g[:, :], dt_g[:, :], ta[:, :]), deps=[e1, e4])
                    e6 = E.emit("scalar", lambda sc: sc.activation(
                        lndt_g[:, :], dt_g[:, :], AF.Ln), deps=[e5])
                    e7 = E.emit("vector", lambda v: v.tensor_scalar(
                        g_g[:, :], dt_g[:, :], acol[:, 0:1], None, ALU.mult),
                        deps=[e5])
                    prev_ev = [e5]
                    dt_done = [e5, e6, e7]
            inproj_done = prev_ev + dt_done

            if not (M_STAGES & 1):
                seq_prev = inproj_done
                continue
            conv_done = []
            pc = []
            for c in range(NBC):
                t0 = E.emit("gpsimd", (lambda c=c: lambda g_: g_.tensor_scalar(
                    cacc1[:, :], xbc[:, c, 0:GT], convw_b[:, c, 0:1], None,
                    ALU.mult))(), deps=inproj_done + pc)
                t1 = E.emit("vector", (lambda c=c: lambda v: v.scalar_tensor_tensor(
                    cacc2[:, :], xbc[:, c, 1:1 + GT], convw_b[:, c, 1:2],
                    cacc1[:, :], ALU.mult, ALU.add))(), deps=[t0])
                t2 = E.emit("vector", (lambda c=c: lambda v: v.scalar_tensor_tensor(
                    cacc1[:, :], xbc[:, c, 2:2 + GT], convw_b[:, c, 2:3],
                    cacc2[:, :], ALU.mult, ALU.add))(), deps=[t1])
                t3 = E.emit("vector", (lambda c=c: lambda v: v.scalar_tensor_tensor(
                    cacc2[:, :], xbc[:, c, 3:3 + GT], convw_b[:, c, 3:4],
                    cacc1[:, :], ALU.mult, ALU.add))(), deps=[t2])
                sl = E.emit("scalar", (lambda c=c: lambda sc: sc.activation(
                    xcs[:, c, :], cacc2[:, :], AF.Silu, bias=convb_b[:, c:c + 1]))(),
                    deps=[t3])
                pc = [sl]
                conv_done.append(sl)
            hcp = E.emit("vector", lambda v: v.tensor_copy(
                xbc[:, :, 0:3], xbc[:, :, GT:GT + 3]),
                deps=conv_done)

            ssd_prev = list(conv_done) + dt_done
            if not (M_STAGES & 2):
                seq_prev = ssd_prev + [hcp]
                continue
            for ck in range(CPG):
                cs = slice(ck * 128, (ck + 1) * 128)
                tg = E.emit("tensor", (lambda cs=cs: lambda t_: t_.matmul(
                    v_trf[:, 0:64], g_g[0:64, cs], identf_b[0:64, 0:64],
                    is_transpose=True, start=True, stop=True))(), deps=ssd_prev)
                eg = E.emit("vector", lambda v: v.tensor_copy(gT[:, :], v_trf[:, 0:64]),
                            deps=[tg])
                tl = E.emit("tensor", (lambda cs=cs: lambda t_: t_.matmul(
                    v_trf[:, 0:64], lndt_g[0:64, cs], identf_b[0:64, 0:64],
                    is_transpose=True, start=True, stop=True))(), deps=[eg] + ssd_prev)
                el = E.emit("vector", lambda v: v.tensor_copy(lndtT[:, :], v_trf[:, 0:64]),
                            deps=[tl])
                xt_evs = []
                pe = [el]
                for q in range(NXC // 4):
                    txs = []
                    for j in range(4):
                        c = 4 * q + j
                        txs.append(E.emit("tensor", (lambda c=c, j=j, cs=cs: lambda t_: t_.matmul(
                            ps_trb[:, j * 128:(j + 1) * 128], xcs[:, c, cs],
                            identb_b[:, :],
                            is_transpose=True, start=True, stop=True))(),
                            deps=ssd_prev + pe[-1:]))
                    ex = E.emit("vector", (lambda q=q: lambda v: v.tensor_copy(
                        x_tm[:, 4 * q:4 * q + 4, :],
                        ps_trb[:, 0:512].rearrange("p (a b) -> p a b", a=4)))(),
                        deps=[txs[-1]])
                    pe = [ex]
                    xt_evs += [ex, ex, ex, ex]
                tb = E.emit("tensor", (lambda cs=cs: lambda t_: t_.matmul(
                    ps_trb[:, 0:128], xcs[:, 32, cs], identb_b[:, :],
                    is_transpose=True, start=True, stop=True))(), deps=pe)
                eb = E.emit("vector", lambda v: v.tensor_copy(b_tm[:, :], ps_trb[:, 0:128]),
                            deps=[tb])
                m_acs = E.emit("tensor", lambda t_: t_.matmul(
                    v_acs[:, 0:64], trif_b[:, :], gT[:, :], start=True, stop=True),
                    deps=[eg, el])
                e_acs = E.emit("vector", lambda v: v.tensor_copy(acsT[:, :], v_acs[:, 0:64]),
                               deps=[m_acs])
                e_eacs = E.emit("scalar", lambda sc: sc.activation(
                    eacsT[:, :], acsT[:, :], AF.Exp), deps=[e_acs])
                m_d1 = E.emit("tensor", lambda t_: t_.matmul(
                    v_dec[:, 0:64], e127_b[:, :], acsT[:, :], start=True, stop=False),
                    deps=[e_acs])
                m_d2 = E.emit("tensor", lambda t_: t_.matmul(
                    v_dec[:, 0:64], identnegf_b[:, :], acsT[:, :],
                    start=False, stop=False), deps=[m_d1])
                m_d3 = E.emit("tensor", lambda t_: t_.matmul(
                    v_dec[:, 0:64], identf_b[:, :], lndtT[:, :],
                    start=False, stop=True), deps=[m_d2, el])
                e_dec = E.emit("scalar", lambda sc: sc.activation(
                    decT[:, :], v_dec[:, 0:64], AF.Exp), deps=[m_d3])
                m_cb = E.emit("tensor", (lambda cs=cs: lambda t_: t_.matmul(
                    ps_dcb[:, 0:128], xcs[:, 32, cs], xcs[:, 33, cs],
                    start=True, stop=True))(), deps=ssd_prev + [eb])
                e_cb = E.emit("vector", lambda v: v.tensor_copy(cb_sb[:, :],
                                                               ps_dcb[:, 0:128]),
                              deps=[m_cb])
                e_hbf = E.emit("vector", lambda v: v.tensor_copy(hbf[:, :], hst[:, :]),
                               deps=ssd_prev + [eb])

                p_done = {}
                d_prev = [e_cb]
                for h4 in range(16):
                    h0 = h4 * 4
                    dmm = []
                    dmm.append(E.emit("tensor", (lambda h0=h0: lambda t_: t_.matmul(
                        ps_dcb[:, 0:512], trinegf_b[:, :],
                        gT[:, h0:h0 + 4].unsqueeze(2).broadcast_to([128, 4, 128]),
                        start=True, stop=False))(), deps=[eg] + d_prev))
                    for j in range(4):
                        h = h4 * 4 + j
                        jsl = slice(j * 128, (j + 1) * 128)
                        dmm.append(E.emit("tensor", (lambda h=h, jsl=jsl: lambda t_: t_.matmul(
                            ps_dcb[:, jsl], gT[:, h:h + 1].broadcast_to([128, 128]),
                            trif_b[:, :], start=False, stop=False))(),
                            deps=[]))
                    dmm.append(E.emit("tensor", (lambda h0=h0: lambda t_: t_.matmul(
                        ps_dcb[:, 0:512], identf_b[:, :],
                        lndtT[:, h0:h0 + 4].unsqueeze(2).broadcast_to([128, 4, 128]),
                        start=False, stop=True))(), deps=[el]))
                    e_x = E.emit("scalar", (lambda: lambda sc: sc.activation(
                        em_sb[:, :], ps_dcb[:, 0:512], AF.Exp))(), deps=[dmm[-1]])
                    msk = E.emit("gpsimd", (lambda: lambda g_: g_.affine_select(
                        em_sb[:, :], em_sb[:, :], [[0, 4], [1, 128]], ALU.is_ge,
                        0.0, base=0, channel_multiplier=-1))(), deps=[e_x])
                    pmu = E.emit("gpsimd", (lambda: lambda g_: g_.tensor_tensor(
                        p_sb4[:, :], em_sb[:, :],
                        cb_sb[:, :].unsqueeze(1).broadcast_to([128, 4, 128]),
                        ALU.mult))(), deps=[msk, e_cb])
                    mg_ = E.emit("tensor", (lambda h0=h0: lambda t_: t_.matmul(
                        ps_ln[:, 0:256], e127_b[:, :],
                        acsT[:, h0:h0 + 4].unsqueeze(2).broadcast_to([128, 4, 64]),
                        start=True, stop=True))(), deps=[e_acs] + d_prev)
                    gex = E.emit("scalar", (lambda: lambda sc: sc.activation(
                        gam_sb[:, :], ps_ln[:, 0:256], AF.Exp))(),
                        deps=[mg_] + d_prev)
                    d_prev = [pmu, gex]

                    for pj in range(2):
                        c = h4 * 2 + pj
                        he, ho = 4 * h4 + 2 * pj, 4 * h4 + 2 * pj + 1
                        pe_sl = slice((2 * pj) * 128, (2 * pj + 1) * 128)
                        po_sl = slice((2 * pj + 1) * 128, (2 * pj + 2) * 128)
                        mu_ = E.emit("tensor", (lambda c=c, cs=cs: lambda t_: t_.matmul(
                            ps_u2[:, :], xcs[:, 33, cs], hbf[:, c * 128:(c + 1) * 128],
                            start=True, stop=True))(),
                            deps=[e_hbf, pmu] + d_prev + ssd_prev)
                        sc1 = E.emit("scalar", (lambda he=he: lambda sc: sc.activation(
                            yoff_sb[:, 0:64], ps_u2[:, 0:64], AF.Copy,
                            scale=eacsT[:, he:he + 1]))(), deps=[mu_, e_eacs])
                        sc2 = E.emit("scalar", (lambda ho=ho: lambda sc: sc.activation(
                            yoff_sb[:, 64:128], ps_u2[:, 64:128], AF.Copy,
                            scale=eacsT[:, ho:ho + 1]))(), deps=[sc1])
                        tr2 = E.emit("tensor", lambda t_: t_.matmul(
                            ps_trb[:, 0:128], yoff_sb[:, :], identb_b[:, :],
                            is_transpose=True, start=True, stop=True),
                            deps=[sc2] + d_prev)
                        tev2 = E.emit("vector", lambda v: v.tensor_copy(
                            yofft_sb[:, :], ps_trb[:, 0:128]), deps=[tr2])
                        my1 = E.emit("tensor", (lambda c=c, pe_sl=pe_sl: lambda t_: t_.matmul(
                            ps_y[0:64, :], x_tm[:, c, 0:64], p_sb4[:, pe_sl],
                            start=True, stop=True))(),
                            deps=[pmu, xt_evs[c], tr2] + d_prev)
                        my2 = E.emit("tensor", (lambda c=c, po_sl=po_sl: lambda t_: t_.matmul(
                            ps_y[64:128, :], x_tm[:, c, 64:128], p_sb4[:, po_sl],
                            start=True, stop=True))(), deps=[my1])
                        yev = E.emit("vector", (lambda c=c, cs=cs, ck=ck: lambda v: v.scalar_tensor_tensor(
                            y_g[:, c, cs], xcs[:, c, cs], drep_b[:, c:c + 1],
                            ps_y[:, :], ALU.mult, ALU.add))(), deps=[my2, tev2])
                        yev2 = E.emit("vector", (lambda c=c, cs=cs: lambda v: v.tensor_add(
                            y_g[:, c, cs], y_g[:, c, cs], yofft_sb[:, :]))(),
                            deps=[yev])
                        xs1 = E.emit("gpsimd", (lambda c=c, he=he: lambda g_: g_.tensor_scalar(
                            xt_sb[:, 0:64], x_tm[:, c, 0:64], decT[:, he:he + 1],
                            None, ALU.mult))(), deps=[e_dec, xt_evs[c], yev2])
                        xs2 = E.emit("gpsimd", (lambda c=c, ho=ho: lambda g_: g_.tensor_scalar(
                            xt_sb[:, 64:128], x_tm[:, c, 64:128], decT[:, ho:ho + 1],
                            None, ALU.mult))(), deps=[xs1])
                        ms_ = E.emit("tensor", lambda t_: t_.matmul(
                            ps_s[:, :], b_tm[:, :], xt_sb[:, :],
                            start=True, stop=True), deps=[xs2, eb])
                        hu1 = E.emit("vector", (lambda c=c, pj=pj: lambda v: v.tensor_mul(
                            htmp[:, :], hst[:, c * 128:(c + 1) * 128],
                            gam_sb[:, pj * 128:(pj + 1) * 128]))(),
                            deps=[gex, e_hbf, ms_])
                        hu2 = E.emit("vector", (lambda c=c: lambda v: v.tensor_add(
                            hst[:, c * 128:(c + 1) * 128], htmp[:, :], ps_s[:, :]))(),
                            deps=[hu1, ms_])
                        d_prev = [hu2]
                ssd_prev = [d_prev[-1] if d_prev else None] + [yev2]
                ssd_prev = [x for x in ssd_prev if x is not None]

            if not (M_STAGES & 4):
                seq_prev = ssd_prev + [hcp]
                continue
            rms_mm = []
            sq_prev = []
            for c in range(NXC):
                s1 = E.emit("scalar", (lambda c=c: lambda sc: sc.activation(
                    em_sb[:, 0:GT], y_g[:, c, :], AF.Square))(),
                    deps=ssd_prev + sq_prev)
                m1 = E.emit("tensor", (lambda c=c: lambda t_: t_.matmul(
                    ps["stat1"][0:1, 0:GT // 2], nc_consts["onescol_sb"][:, 0:1],
                    em_sb[:, 0:GT // 2], start=(c == 0), stop=(c == NXC - 1)))(),
                    deps=[s1])
                m2 = E.emit("tensor", (lambda c=c: lambda t_: t_.matmul(
                    ps["stat2"][0:1, 0:GT // 2], nc_consts["onescol_sb"][:, 0:1],
                    em_sb[:, GT // 2:GT], start=(c == 0), stop=(c == NXC - 1)))(),
                    deps=[m1])
                sq_prev = [m2]
                rms_mm.append(m2)
            r1 = E.emit("vector", lambda v: v.tensor_scalar(
                rms_r[0:1, 0:GT // 2], ps["stat1"][0:1, 0:GT // 2], 1.0 / DIN,
                RMS_EPS, ALU.mult, ALU.add), deps=[rms_mm[-1]])
            r1b = E.emit("vector", lambda v: v.tensor_scalar(
                rms_r[0:1, GT // 2:GT], ps["stat2"][0:1, 0:GT // 2], 1.0 / DIN,
                RMS_EPS, ALU.mult, ALU.add), deps=[rms_mm[-1], r1])
            r2 = E.emit("scalar", lambda sc: sc.activation(
                rms_r[0:1, :], rms_r[0:1, :], AF.Sqrt), deps=[r1b])
            r3 = E.emit("vector", lambda v: v.reciprocal(
                rms_r[0:1, :], rms_r[0:1, :]), deps=[r2])
            rb = E.emit("tensor", lambda t_: t_.matmul(
                v_rms[:, :], nc_consts["onesrow_sb"][0:1, :], rms_r[0:1, :],
                start=True, stop=True), deps=[r3])
            g_done = []
            prev_gev = []
            zlast = {0: prev_w, 1: prev_w}
            szp = {0: [rb], 1: [rb]}
            gtp = [None, None]
            for zm in range(NXC):
                b = zm % 2
                pp = ps_proj if b == 0 else ps_dcb
                d_w = E.dma("gpsimd", wbuf[:, b, 0:16, :],
                            dr["w_z"][zm], deps=zlast[b])
                g1 = E.emit("vector", (lambda zm=zm, b=b: lambda v: v.tensor_mul(
                    gtmp[:, b, :], y_g[:, zm, :], v_rms[:, :]))(),
                    deps=[rb] + ssd_prev + (g_done[zm - 2:zm - 1] or []))
                mm = []
                for ki in range(NCHUNK):
                    mm.append(E.emit("tensor", (lambda zm=zm, ki=ki, b=b, pp=pp: lambda t_: t_.matmul(
                        pp[:, 0:GT], wbuf[:, b, ki, :], ub[:, ki, :],
                        start=(ki == 0), stop=(ki == NCHUNK - 1)))(),
                        deps=[d_w, d_u] + szp[b]))
                zlast[b] = [mm[-1]]
                prev_w = [mm[-1]]
                sz = E.emit("scalar", (lambda zm=zm, b=b, pp=pp: lambda sc: sc.activation(
                    ztmp[:, b, :], pp[:, 0:GT], AF.Silu))(),
                    deps=[mm[-1]] + (g_done[zm - 2:zm - 1] or []))
                szp[b] = [sz]
                gev = E.emit("vector", (lambda zm=zm, b=b: lambda v: v.scalar_tensor_tensor(
                    y_g[:, zm, :], ztmp[:, b, :], nw_b[:, zm:zm + 1], gtmp[:, b, :],
                    ALU.mult, ALU.mult))(), deps=[sz, g1])
                prev_gev = [gev]
                g_done.append(gev)
            def dma_uf(eng, base=base, lv=lv):
                return eng.dma_start(
                    out=uf[:, :, :],
                    in_=xin[:, bass.ds(base + lv.val * GT, GT)].rearrange(
                        "(c p) n -> p c n", p=128))
            d_uf = E.emit("sync", dma_uf, deps=g_done, inc=16)
            s_steps = []
            olast = {0: prev_w, 1: prev_w}
            for m in range(NCHUNK):
                b = m % 2
                pp = ps_proj if b == 0 else ps_dcb
                d_w = E.dma("gpsimd", wbuf[:, b, :, :],
                            dr["w_o"][m],
                            deps=olast[b])
                mm = []
                for ki in range(NXC):
                    mm.append(E.emit("tensor", (lambda m=m, ki=ki, b=b, pp=pp: lambda t_: t_.matmul(
                        pp[:, 0:GT], wbuf[:, b, ki, :], y_g[:, ki, :],
                        start=(ki == 0), stop=(ki == NXC - 1)))(),
                        deps=[d_w] + g_done + (s_steps[m - 2:m - 1] or [])))
                olast[b] = [mm[-1]]
                prev_w = [mm[-1]]
                ev = E.emit("vector", (lambda m=m, pp=pp: lambda v: v.tensor_add(
                    uf[:, m, :], uf[:, m, :], pp[:, 0:GT]))(),
                    deps=[mm[-1], d_uf])
                s_steps.append(ev)
            s_chunks = [uf[:, c, :] for c in range(NCHUNK)]
            ln_outs = emit_layernorm(E, sb, ps, GT, s_chunks, s_chunks,
                                     s_steps, nc_consts)
            def dma_out2(eng, base=base, lv=lv):
                return eng.dma_start(
                    out=xout[:, bass.ds(base + lv.val * GT, GT)].rearrange(
                        "(c p) n -> p c n", p=128),
                    in_=uf[:, :, :])
            d_out = E.emit("sync", dma_out2, deps=ln_outs, inc=16)
            seq_prev = [d_out] + ln_outs + [hcp]
        E.end_loop()
        E.steps[d_u]["deps"] += [(x, -1) for x in seq_prev]
        lp = E.loops[-1]
        gdmas = [i for i in range(lp["start"], lp["end"])
                 if E.steps[i]["engine"] == "gpsimd"
                 and E.steps[i]["inc"] == 16 and i > d_u]
        E.steps[gdmas[0]]["deps"] += [(prev_w[0], -1)]
        E.steps[gdmas[1]]["deps"] += [(prev_w[0], -1)]
        seq_prev = [(x, 0) for x in seq_prev]
        seq_prev = [x[0] for x in seq_prev]
        fin_prev = seq_prev
    return fin_prev


def build_mamba_program():
    nc = bass.Bass("TRN2", target_bir_lowering=False, debug=False)
    xin = nc.dram_tensor("xin", [D_MODEL, TOK], F32, kind="ExternalInput").ap()
    xout = nc.dram_tensor("xout", [D_MODEL, TOK], F32, kind="ExternalOutput").ap()
    dr = mamba_dram(nc)
    onescol = nc.inline_tensor(np.ones((128, 1), np.float32), name="onescol_c")
    onesrow = nc.inline_tensor(np.ones((1, 128), np.float32), name="onesrow_c")
    with ExitStack() as es:
        onescol_sb = es.enter_context(nc.sbuf_tensor("onescol", [128, 1], F32))
        onesrow_sb = es.enter_context(nc.sbuf_tensor("onesrow", [1, 128], F32))
        nc_consts = dict(onescol_sb=onescol_sb, onesrow_sb=onesrow_sb)
        E = Emit(serial=False)
        E.dma("sync", onescol_sb[:, :], onescol.ap())
        E.dma("sync", onesrow_sb[0:1, :], onesrow.ap())
        fin = build_mamba(nc, E, es, xin, xout, dr, nc_consts)
        def fini(eng):
            return eng.nop()
        E.emit("sync", fini, deps=fin)
        sems, body = E.lower(nc, es)
        block = es.enter_context(nc.Block())
        block.tensor(lambda t: body("tensor", t))
        block.vector(lambda v: body("vector", v))
        block.scalar(lambda sc: body("scalar", sc))
        block.gpsimd(lambda g: body("gpsimd", g))
        block.sync(lambda s: body("sync", s))
    return nc


def build_fused_program():
    nc = bass.Bass("TRN2", target_bir_lowering=False, debug=False)
    xin = nc.dram_tensor("xin", [D_MODEL, TOK], F32, kind="ExternalInput").ap()
    xout = nc.dram_tensor("xout", [D_MODEL, TOK], F32, kind="ExternalOutput").ap()
    import os as _os
    _dbg = _os.environ.get("FUSED_DBG") == "1"
    _ki = "ExternalOutput" if _dbg else "Internal"
    x1 = nc.dram_tensor("x1", [D_MODEL, TOK], F32, kind=_ki).ap()
    x2 = nc.dram_tensor("x2", [D_MODEL, TOK], F32, kind=_ki).ap()
    x3 = nc.dram_tensor("x3", [D_MODEL, TOK], F32, kind=_ki).ap()
    drA = mamba_dram(nc)
    drC = attn1_dram(nc)
    d = {}
    def inp(name, shape, dtype=F32):
        d[name] = nc.dram_tensor(name, list(shape), dtype, kind="ExternalInput").ap()
    inp("wg0", [64, 128, 16, 128], I8); inp("wu0", [64, 128, 16, 128], I8)
    inp("wd0", [16, 128, 64, 128], I8)
    inp("sg0", [128, 64]); inp("su0", [128, 64]); inp("sd0", [128, 16])
    inp("wg1", [44, 128, 16, 128]); inp("wu1", [44, 128, 16, 128])
    inp("wd1", [16, 128, 44, 128])
    inp("sg1", [128, 44]); inp("su1", [128, 44]); inp("sd1", [128, 16])
    onescol = nc.inline_tensor(np.ones((128, 1), np.float32), name="onescol_c")
    onesrow = nc.inline_tensor(np.ones((1, 128), np.float32), name="onesrow_c")
    with ExitStack() as es:
        onescol_sb = es.enter_context(nc.sbuf_tensor("onescol", [128, 1], F32))
        onesrow_sb = es.enter_context(nc.sbuf_tensor("onesrow", [1, 128], F32))
        nc_consts = dict(onescol_sb=onescol_sb, onesrow_sb=onesrow_sb)
        E = Emit(serial=False)
        c1 = E.dma("sync", onescol_sb[:, :], onescol.ap())
        c2 = E.dma("sync", onesrow_sb[0:1, :], onesrow.ap())
        with ExitStack() as esA:
            build_mamba(nc, E, esA, xin, x1, drA, nc_consts, deps0=[c1, c2])
            t1 = E.tail()
        with ExitStack() as esB:
            build_ffn(nc, E, esB, x1, x2, d["wg0"], d["wu0"], d["wd0"],
                      d["sg0"], d["su0"], d["sd0"], nc_consts,
                      NI=64, prefix="f0_", deps0=t1)
            t2 = E.tail()
        with ExitStack() as esC:
            build_attn1(nc, E, esC, x2, x3, drC, nc_consts, deps0=t2)
            t3 = E.tail()
        with ExitStack() as esD:
            build_ffn(nc, E, esD, x3, xout, d["wg1"], d["wu1"], d["wd1"],
                      d["sg1"], d["su1"], d["sd1"], nc_consts,
                      NI=44, prefix="f1_", deps0=t3)
            t4 = E.tail()
        def fini(eng):
            return eng.nop()
        E.emit("sync", fini, deps=t4)
        sems, body = E.lower(nc, es)
        block = es.enter_context(nc.Block())
        block.tensor(lambda t: body("tensor", t))
        block.vector(lambda v: body("vector", v))
        block.scalar(lambda sc: body("scalar", sc))
        block.gpsimd(lambda g: body("gpsimd", g))
        block.sync(lambda s: body("sync", s))
    return nc


_PROGS = {}


def _get_prog(name):
    if name not in _PROGS:
        _PROGS[name] = {"mamba": build_mamba_program,
                        "ffn0": build_ffn0_program,
                        "attn1": build_attn1_program,
                        "ffn1": build_ffn1_program,
                        "fused": build_fused_program}[name]()
    return _PROGS[name]


def kernel(**inputs):
    n = 8
    u = np.asarray(inputs["L0_attn_u"])
    xins = []
    for c in range(n):
        xins.append(np.ascontiguousarray(
            np.concatenate([u[2 * c].T, u[2 * c + 1].T], axis=1)))

    wmap = {}
    wmap.update(prep_mamba(inputs))
    wmap.update(prep_ffn0(inputs, 0, 0))
    wmap.update(prep_attn1(inputs))
    wmap.update(prep_ffn1(inputs))

    nc = _get_prog("fused")
    in_maps = [dict(xin=xins[c], **wmap) for c in range(n)]
    res = run_bass_kernel_spmd(nc, in_maps, core_ids=list(range(n)))
    xs = [res.results[c]["xout"] for c in range(n)]

    out = np.empty((16, SEQ, D_MODEL), np.float32)
    for c in range(n):
        out[2 * c] = xs[c][:, 0:SEQ].T
        out[2 * c + 1] = xs[c][:, SEQ:].T
    return out
